# version 10
# baseline (speedup 1.0000x reference)
import numpy as np
from contextlib import ExitStack
import concourse.bass as bass
import concourse.mybir as mybir
from concourse.bass_utils import run_bass_kernel_spmd
from concourse.aot_env import allow_ici_all_to_all

F32 = mybir.dt.float32
BF16 = mybir.dt.bfloat16
AF = mybir.ActivationFunctionType
ALU = mybir.AluOpType
AX = mybir.AxisListType

D = 4096
NH, CD, NB = 16, 32, 8
NCORE = 8
JCORE = D // NCORE
HPC = 2
TB = 512
NDC = D // 128
SCALE = 1.0 / 16.0
EPS = 1e-5

ENGINES = ("sync", "tensor", "vector", "scalar", "gpsimd")


def _ga_consts():
    g = [1.0, 1.0, 1.0]
    dim, n = 3, 8
    masks = sorted(range(n), key=lambda m: (bin(m).count("1"), m))
    inv = {m: i for i, m in enumerate(masks)}
    C = np.zeros((n, n, n), dtype=np.float64)
    for i, a in enumerate(masks):
        for j, b in enumerate(masks):
            sw, at = 0, a >> 1
            while at:
                sw += bin(at & b).count("1")
                at >>= 1
            s = -1.0 if sw & 1 else 1.0
            for t in range(dim):
                if a >> t & 1 and b >> t & 1:
                    s *= g[t]
            C[i, j, inv[a ^ b]] = s
    grades = np.array([bin(m).count("1") for m in masks])
    rev = np.where(grades * (grades - 1) // 2 % 2 == 1, -1.0, 1.0)
    biv = C[:, :, [4, 5, 6]]
    d0 = np.zeros((128, 128))
    kcx = np.zeros((3, 128, 128))
    for c16 in range(16):
        for i in range(8):
            j = c16 * 8 + i
            d0[j, j] = rev[i] * SCALE
            for i2 in range(8):
                for cc in range(3):
                    kcx[cc, j, c16 * 8 + i2] = rev[i] * biv[i, i2, cc]
    return d0.astype(np.float32), kcx.astype(np.float32)


class Plan:
    def __init__(self):
        self.items = []
        self.counts = {}
        self.last_wait = {}

    def op(self, eng, emit, waits=(), incs=()):
        idx = len(self.items)
        wl = []
        for tok in waits:
            if tok is None:
                continue
            s, v, si = tok
            assert si < idx
            k = (eng, s)
            if v > self.last_wait.get(k, 0):
                self.last_wait[k] = v
                wl.append((s, v))
        incs = list(incs)
        assert len(incs) <= 1
        toks = []
        for s, amt in incs:
            self.counts[s] = self.counts.get(s, 0) + amt
            toks.append((s, self.counts[s], idx))
        self.items.append((eng, emit, wl, incs))
        return toks[0] if toks else None

    def emit(self, eng_name, e, sems):
        for engn, emit, wl, incs in self.items:
            if engn != eng_name:
                continue
            for s, v in wl:
                e.wait_ge(sems[s], v)
            if emit is None:
                continue
            instr = emit(e)
            for s, amt in incs:
                instr.then_inc(sems[s], amt)


def build(T=8192, phases=("p0", "q", "k", "v", "p2", "cc", "p3"), taps=(), tb3_override=None):
    NTB = T // TB
    TC = T // NCORE
    TB3 = tb3_override or min(TB, TC)
    NT3 = TC // TB3
    NQB = T // TB
    NMC = T // 128
    NG = NMC // 2
    NTC3 = TB3 // 128

    nc = bass.Bass("TRN2", target_bir_lowering=False, debug=False)

    x_fm = nc.dram_tensor("x_fm", [NTB * 128, NDC * TB], F32, kind="ExternalInput").ap()
    x_sl = nc.dram_tensor("x_sl", [128, NDC * TC], F32, kind="ExternalInput").ap()
    wqT = nc.dram_tensor("wqT", [D, JCORE], F32, kind="ExternalInput").ap()
    wkT = nc.dram_tensor("wkT", [D, JCORE], F32, kind="ExternalInput").ap()
    wvT = nc.dram_tensor("wvT", [D, JCORE], F32, kind="ExternalInput").ap()
    bqc = nc.dram_tensor("bqc", [128, 4], F32, kind="ExternalInput").ap()
    bkc = nc.dram_tensor("bkc", [128, 4], F32, kind="ExternalInput").ap()
    bvc = nc.dram_tensor("bvc", [128, 4], F32, kind="ExternalInput").ap()
    bwt = nc.dram_tensor("bwt", [1, HPC * 3], F32, kind="ExternalInput").ap()
    wbT = nc.dram_tensor("wbT", [D, D], F32, kind="ExternalInput").ap()
    bbc = nc.dram_tensor("bbc", [128, NDC], F32, kind="ExternalInput").ap()
    wrT = nc.dram_tensor("wrT", [D, 8], F32, kind="ExternalInput").ap()
    aT = nc.dram_tensor("aT", [D, 128], F32, kind="ExternalInput").ap()
    b2dT = nc.dram_tensor("b2dT", [128, D], F32, kind="ExternalInput").ap()
    d0kc = nc.dram_tensor("d0kc", [128, 4 * 128], F32, kind="ExternalInput").ap()
    identF = nc.dram_tensor("identF", [128, 128], F32, kind="ExternalInput").ap()
    e2c = nc.dram_tensor("e2c", [8, 128], F32, kind="ExternalInput").ap()
    out = nc.dram_tensor("out", [D, TC], F32, kind="ExternalOutput").ap()

    xbf = nc.dram_tensor("xbf", [NTB * 128, NDC * TB], BF16)
    mr_rows = nc.dram_tensor("mr_rows", [2 * NTB, TB], F32).ap()
    qeffsp = nc.dram_tensor("qeffsp", [2 * 128, T], BF16).ap()
    ksp = nc.dram_tensor("ksp", [2 * 128, T], BF16).ap()
    vsp = nc.dram_tensor("vsp", [2 * 128, NMC * 256], BF16).ap()
    HTC = TC // 2
    a2a_in1 = nc.dram_tensor("a2a_in1", [D, HTC], BF16)
    a2a_in2 = nc.dram_tensor("a2a_in2", [D, HTC], BF16)
    a2a_out1 = nc.dram_tensor("a2a_out1", [D, HTC], BF16)
    a2a_out2 = nc.dram_tensor("a2a_out2", [D, HTC], BF16)
    x2sp = nc.dram_tensor("x2sp", [128, NDC * TC], BF16).ap()
    wbbf = nc.dram_tensor("wbbf", [D, D], BF16).ap()
    xbf_ap = xbf.ap()
    a2a_in_aps = [a2a_in1.ap(), a2a_in2.ap()]
    a2a_out_aps = [a2a_out1.ap(), a2a_out2.ap()]

    tap_aps = {}
    for t_ in taps:
        shp, dt_ = {
            "qeffsp": ([2 * 128, T], BF16), "ksp": ([2 * 128, T], BF16),
            "vsp": ([2 * 128, NMC * 256], BF16), "mrows": ([2 * NTB, TB], F32),
            "xbf": ([NTB * 128, NDC * TB], BF16), "x2sp": ([128, NDC * TC], BF16),
        }[t_]
        tap_aps[t_] = nc.dram_tensor("tap_" + t_, shp, dt_, kind="ExternalOutput").ap()

    P = Plan()
    sem_names = ["dXA", "dXB", "dWG", "dMR", "dRR", "dWBA", "dWBB",
                 "dXIA", "dXIB", "dAIA", "dAIB", "dP2", "dCST", "dC3",
                 "dSP", "dOUT", "sCC", "sPE", "sV", "sSC", "sGP"]

    with ExitStack() as ctx:
        sems = {n: ctx.enter_context(nc.semaphore(n)) for n in sem_names}

        def sb(name, shape, dt):
            return ctx.enter_context(nc.sbuf_tensor(name, shape, dt))

        def psum(name, shape, dt):
            return ctx.enter_context(nc.psum_tensor(name, shape, dt))

        AR1 = sb("AR1", [128, 32768], BF16)
        AR2 = sb("AR2", [128, 16384], BF16)
        AR3 = sb("AR3", [128, 16384], BF16)
        AR4 = sb("AR4", [128, 16384], BF16)
        xbv = AR1[:].rearrange("p (s c t) -> p s c t", s=2, c=NDC)
        pbv = AR1[:, 0:NMC * TB].rearrange("p (m t) -> p m t", t=TB)
        x2v = AR1[:, 0:NDC * TC].rearrange("p (c t) -> p c t", t=TC)
        wgt = AR2[:].rearrange("p (c j) -> p c j", c=NDC)
        vh = AR2[:, 0:NMC * 256].rearrange("p (m j) -> p m j", j=256)
        wbs = AR2[:].rearrange("p (s c j) -> p s c j", s=2, c=NDC)
        sq1 = AR3[:].rearrange("p (c t) -> p c t", c=NDC)
        kh = AR3[:, 0:2 * T].rearrange("p (a t) -> p a t", a=2)
        aTb = AR3[:, 0:NDC * 128].rearrange("p (c j) -> p c j", c=NDC)
        b2dTb = AR3[:, NDC * 128:NDC * 128 + D]
        qeh = AR4[:, 0:2 * T].rearrange("p (a t) -> p a t", a=2)
        sq3 = AR4[:, 0:NDC * TB3].rearrange("p (c t) -> p c t", c=NDC)

        mrl = sb("mrl", [1, 2, TB], F32)
        rrl = sb("rrl", [1, 2, TB], F32)
        mrow = mrl[0:1, 0, :]
        rrow = rrl[0:1, 0, :]
        srow0 = mrl[0:1, 1, :]
        srow1 = rrl[0:1, 1, :]
        wsum_neg = sb("wsum_neg", [1, JCORE], F32)
        rbc = sb("rbc", [128, TB], F32)
        rbc2 = sb("rbc2", [128, TB], F32)
        rcols = sb("rcols", [128, 4], F32)
        qsc = sb("qsc", [128, 2, TB], BF16)
        fsc = sb("fsc", [128, 2, TB], F32)
        stgb = sb("stgb", [128, 2, 2, TB], BF16)
        ost = sb("ost", [128, 2, TB3], F32)
        identf = sb("identf", [128, 128], F32)
        d0kcs = sb("d0kcs", [128, 4, 128], BF16)
        bmat = sb("bmat", [128, 2, 128], BF16)
        bmatf = sb("bmatf", [128, 2, 128], F32)
        bwcol = sb("bwcol", [128, 6], F32)
        bbqcol = sb("bbqcol", [128, 4], F32)
        bqs = sb("bqs", [128, 4], F32)
        bks = sb("bks", [128, 4], F32)
        bvs = sb("bvs", [128, 4], F32)
        bbs = sb("bbs", [128, NDC], F32)
        bws = sb("bws", [1, HPC * 3], F32)
        e2b = sb("e2b", [8, 128], BF16)
        wrTb = sb("wrTb", [128, NDC, 8], BF16)
        hs = sb("hs", [128, TB3], F32)
        hwb = sb("hwb", [128, 2, TB3], BF16)
        rl = sb("rl", [8, TB3], F32)
        wem = sb("wem", [8, TB3], BF16)
        lt8 = sb("lt8", [128, 8], F32)
        l28 = sb("l28", [128, 8], F32)
        ge1 = sb("ge1", [128, 8], F32)
        ge2 = sb("ge2", [128, 8], F32)
        wtm = sb("wtm", [128, 8], F32)
        c1 = sb("c1", [128, 1], F32)
        c2 = sb("c2", [128, 1], F32)
        c3 = sb("c3", [128, 1], F32)
        c4 = sb("c4", [128, 1], F32)
        onesb = sb("onesb", [128, 1], BF16)
        onesf = sb("onesf", [1, 128], F32)
        epst = sb("epst", [1, 1], F32)

        ps0 = psum("ps0", [128, 2048], F32)
        ps4 = psum("ps4", [128, 1024], F32)
        ps6 = psum("ps6", [128, 512], F32)
        ps7 = psum("ps7", [128, 512], F32)

        t_identf = P.op("sync", lambda e: e.dma_start(out=identf[:], in_=identF), incs=[("dCST", 16)])
        t_d0kc = P.op("gpsimd", lambda e: e.dma_start(
            out=d0kcs[:].rearrange("p a b -> p (a b)"), in_=d0kc), incs=[("dCST", 16)])
        t_bq = P.op("sync", lambda e: e.dma_start(out=bqs[:], in_=bqc), incs=[("dCST", 16)])
        t_bk = P.op("sync", lambda e: e.dma_start(out=bks[:], in_=bkc), incs=[("dCST", 16)])
        t_bv = P.op("sync", lambda e: e.dma_start(out=bvs[:], in_=bvc), incs=[("dCST", 16)])
        t_bb = P.op("sync", lambda e: e.dma_start(out=bbs[:], in_=bbc), incs=[("dCST", 16)])
        t_bw = P.op("sync", lambda e: e.dma_start(out=bws[:], in_=bwt), incs=[("dCST", 16)])
        t_e2 = P.op("gpsimd", lambda e: e.dma_start(out=e2b[:], in_=e2c), incs=[("dCST", 16)])
        t_cst_all = ("dCST", P.counts.get("dCST", 0), len(P.items) - 1)
        t_identf = t_d0kc = t_bq = t_bk = t_bv = t_bb = t_bw = t_e2 = t_cst_all
        t_ones = P.op("vector", lambda e: e.memset(onesb[:], 1.0), incs=[("sV", 1)])
        t_onesf = P.op("vector", lambda e: e.memset(onesf[:], 1.0), incs=[("sV", 1)])
        t_eps = P.op("vector", lambda e: e.memset(epst[:], EPS), incs=[("sV", 1)])

        t_bm = [None, None]
        t_bbq = None
        if "p0" in phases:
            t_bwc = P.op("tensor", lambda e: e.matmul(
                ps6[:, 0:6], onesf[:], bws[0:1, 0:6], start=True, stop=True),
                waits=[t_onesf, t_bw], incs=[("sPE", 1)])
            t_bwcp = P.op("vector", lambda e: e.tensor_copy(out=bwcol[:], in_=ps6[:, 0:6]),
                          waits=[t_bwc], incs=[("sV", 1)])
            for h in range(HPC):
                tt0 = P.op("vector", lambda e, h=h: e.scalar_tensor_tensor(
                    out=bmatf[:, h, :], in0=d0kcs[:, 1, :], scalar=bwcol[:, h * 3:h * 3 + 1],
                    in1=d0kcs[:, 0, :], op0=ALU.mult, op1=ALU.add),
                    waits=[t_bwcp, t_d0kc], incs=[("sV", 1)])
                tt1 = P.op("vector", lambda e, h=h: e.scalar_tensor_tensor(
                    out=bmatf[:, h, :], in0=d0kcs[:, 2, :], scalar=bwcol[:, h * 3 + 1:h * 3 + 2],
                    in1=bmatf[:, h, :], op0=ALU.mult, op1=ALU.add),
                    waits=[tt0], incs=[("sV", 1)])
                tt2 = P.op("vector", lambda e, h=h: e.scalar_tensor_tensor(
                    out=bmatf[:, h, :], in0=d0kcs[:, 3, :], scalar=bwcol[:, h * 3 + 2:h * 3 + 3],
                    in1=bmatf[:, h, :], op0=ALU.mult, op1=ALU.add),
                    waits=[tt1], incs=[("sV", 1)])
                t_bm[h] = P.op("vector", lambda e, h=h: e.tensor_copy(
                    out=bmat[:, h, :], in_=bmatf[:, h, :]), waits=[tt2], incs=[("sV", 1)])
            t_bbqm = []
            for h in range(HPC):
                t_bbqm.append(P.op("tensor", lambda e, h=h: e.matmul(
                    ps7[:, 2 * h:2 * h + 2], bmatf[:, h, :], bqs[:, 2 * h:2 * h + 2],
                    start=True, stop=True),
                    waits=[t_bm[h], t_bq], incs=[("sPE", 1)]))
            t_bbq = P.op("vector", lambda e: e.tensor_copy(out=bbqcol[:], in_=ps7[:, 0:4]),
                         waits=[t_bbqm[-1]], incs=[("sV", 1)])

        t_mr_save = [None] * (2 * NTB)
        t_xbf_save = [None] * NTB
        t_q_spill, t_k_spill, t_v_spill = [], [], []
        xb_users = [[], []]
        qsc_users = [[], []]
        fsc_users = [[], []]
        stgb_users = [[], []]
        psq_users = [[], []]
        pstat_users = []
        ps4_users = [[], []]
        ps6_users = [t_bwcp] and []
        ps6_users = []
        ps7_users = [t_bbq] if t_bbq else []
        sq_users = []
        mrow_users = []
        rrow_users = []
        srow_users = []
        wgt_users = []
        rcol_users = []

        if "p0" in phases:
            ps6_users.append(t_bwcp)

        def row_chain(tb3_len, save_idx=None):
            nonlocal pstat_users, mrow_users, rrow_users, srow_users, ps7_users
            L = tb3_len
            tm = P.op("vector", lambda e, L=L: e.tensor_scalar(
                out=mrow[0:1, 0:L], in0=ps0[0:1, 0:L], scalar1=1.0 / D, scalar2=None,
                op0=ALU.mult), waits=mrow_users, incs=[("sV", 1)])
            mrow_users = []
            tmsq = P.op("vector", lambda e, L=L: e.tensor_tensor(
                out=srow0[0:1, 0:L], in0=mrow[0:1, 0:L], in1=mrow[0:1, 0:L], op=ALU.mult),
                waits=[tm] + srow_users, incs=[("sV", 1)])
            srow_users = []
            tvar = P.op("vector", lambda e, L=L: e.scalar_tensor_tensor(
                out=srow1[0:1, 0:L], in0=ps0[0:1, 512:512 + L], scalar=1.0 / D,
                in1=srow0[0:1, 0:L], op0=ALU.mult, op1=ALU.subtract),
                waits=[tmsq], incs=[("sV", 1)])
            pstat_users = [tm, tvar]
            tln = P.op("scalar", lambda e, L=L: e.activation(
                out=srow0[0:1, 0:L], in_=srow1[0:1, 0:L], func=AF.Ln, bias=epst[0:1, 0:1],
                scale=1.0), waits=[tvar, t_eps], incs=[("sSC", 1)])
            tr = P.op("scalar", lambda e, L=L: e.activation(
                out=rrow[0:1, 0:L], in_=srow0[0:1, 0:L], func=AF.Exp, scale=-0.5),
                waits=[tln] + rrow_users, incs=[("sSC", 1)])
            rrow_users = []
            srow_users = [tr]
            return tm, tr

        def bcast(row_tok, row_ap, dst_ap, L, extra=()):
            nonlocal ps7_users
            trb = P.op("tensor", lambda e, row_ap=row_ap, L=L: e.matmul(
                ps7[:, 0:L], onesf[:], row_ap, start=True, stop=True),
                waits=[row_tok, t_onesf] + ps7_users, incs=[("sPE", 1)])
            ps7_users = []
            tc_ = P.op("vector", lambda e, dst_ap=dst_ap, L=L: e.tensor_copy(
                out=dst_ap, in_=ps7[:, 0:L]), waits=[trb] + list(extra),
                incs=[("sV", 1)])
            ps7_users.append(tc_)
            return trb, tc_

        def wsum_mms(t_wload):
            nonlocal ps6_users
            tk = None
            for c in range(NDC):
                tk = P.op("tensor", lambda e, c=c: e.matmul(
                    ps6[0:1, 0:512], onesb[:], wgt[:, c, :],
                    start=(c == 0), stop=(c == NDC - 1)),
                    waits=([t_wload, t_ones] + ps6_users) if c == 0 else [],
                    incs=[("sPE", 1)] if c == NDC - 1 else [])
            ps6_users = []
            tneg = P.op("vector", lambda e: e.tensor_scalar(
                out=wsum_neg[:], in0=ps6[0:1, 0:512], scalar1=-1.0, scalar2=None,
                op0=ALU.mult), waits=[tk], incs=[("sV", 1)])
            ps6_users.append(tneg)
            return tneg

        t_xload = [None] * (NTB + 1)

        QB = None
        def qbank(jc):
            return (ps0[:, 1024:1536], ps0[:, 1536:2048],
                    ps4[:, 0:512], ps4[:, 512:1024])[jc]
        psq4 = [[], [], [], []]
        ps6b_users = []
        if "q" in phases:
            t_wq = P.op("gpsimd", lambda e: e.dma_start(
                out=wgt[:].rearrange("p c j -> p (c j)"),
                in_=wqT.rearrange("(c p) j -> p c j", p=128)), incs=[("dWG", 16)])
            t_wqsum = wsum_mms(t_wq)
            t_xload[0] = P.op("gpsimd", lambda e: e.dma_start(
                out=xbv[:, 0, :, :].rearrange("p c t -> p (c t)"),
                in_=x_fm[0:128, :]), incs=[("dXA", 16)])
            for tb in range(NTB):
                sl = tb % 2
                if tb + 1 < NTB:
                    t_xload[tb + 1] = P.op("gpsimd", lambda e, tb=tb: e.dma_start(
                        out=xbv[:, (tb + 1) % 2, :, :].rearrange("p c t -> p (c t)"),
                        in_=x_fm[(tb + 1) * 128:(tb + 2) * 128, :]),
                        waits=xb_users[(tb + 1) % 2],
                        incs=[("dXA" if (tb + 1) % 2 == 0 else "dXB", 16)])
                    xb_users[(tb + 1) % 2] = []
                t_xbf_save[tb] = P.op("sync", lambda e, tb=tb, sl=sl: e.dma_start(
                    out=xbf_ap[tb * 128:(tb + 1) * 128, :],
                    in_=xbv[:, sl, :, :].rearrange("p c t -> p (c t)")),
                    waits=[t_xload[tb]], incs=[("dSP", 16)])
                xb_users[sl].append(t_xbf_save[tb])
                tsq = None
                for c in range(NDC):
                    tsq = P.op("vector", lambda e, c=c, sl=sl: e.tensor_tensor(
                        out=sq1[:, c, :], in0=xbv[:, sl, c, :], in1=xbv[:, sl, c, :],
                        op=ALU.mult),
                        waits=([t_xload[tb]] + sq_users) if c == 0 else [],
                        incs=[("sV", 1)] if c == NDC - 1 else [])
                sq_users = []
                xb_users[sl].append(tsq)
                w0 = pstat_users + [t_xload[tb], t_ones]
                for c in range(NDC):
                    P.op("tensor", lambda e, c=c, sl=sl: e.matmul(
                        ps0[0:1, 0:512], onesb[:], xbv[:, sl, c, :],
                        start=(c == 0), stop=(c == NDC - 1)),
                        waits=w0 if c == 0 else [])
                pstat_users = []
                tmains = [None] * 4
                for c in range(NDC):
                    tk = P.op("tensor", lambda e, c=c, sl=sl: e.matmul(
                        qbank(0), wgt[:, c, 0:128], xbv[:, sl, c, :],
                        start=(c == 0), stop=False),
                        waits=(psq4[0] + [t_wq]) if c == 0 else [],
                        incs=[("sPE", 1)] if c == NDC - 1 else [])
                psq4[0] = []
                tmains[0] = tk
                tstat = None
                for c in range(NDC):
                    tstat = P.op("tensor", lambda e, c=c: e.matmul(
                        ps0[0:1, 512:1024], onesb[:], sq1[:, c, :],
                        start=(c == 0), stop=(c == NDC - 1)),
                        waits=[tsq] if c == 0 else [],
                        incs=[("sPE", 1)] if c == NDC - 1 else [])
                mrow_users.append(tstat)
                sq_users.append(tstat)
                for jc in range(1, 4):
                    for c in range(NDC):
                        tk = P.op("tensor", lambda e, c=c, jc=jc, sl=sl: e.matmul(
                            qbank(jc), wgt[:, c, jc * 128:(jc + 1) * 128], xbv[:, sl, c, :],
                            start=(c == 0), stop=False),
                            waits=psq4[jc] if c == 0 else [],
                            incs=[("sPE", 1)] if c == NDC - 1 else [])
                    psq4[jc] = []
                    tmains[jc] = tk
                tm, tr = row_chain(TB)
                t_mr_save[tb] = P.op("sync", lambda e, tb=tb: e.dma_start(
                    out=mr_rows[tb:tb + 1, :], in_=mrow), waits=[tm], incs=[("dSP", 16)])
                t_mr_save[NTB + tb] = P.op("sync", lambda e, tb=tb: e.dma_start(
                    out=mr_rows[NTB + tb:NTB + tb + 1, :], in_=rrow),
                    waits=[tr], incs=[("dSP", 16)])
                mrow_users.append(t_mr_save[tb])
                rrow_users.append(t_mr_save[NTB + tb])
                trb, trbc = bcast(tr, rrow, rbc[:], TB)
                rrow_users.append(trb)
                trk4 = [None] * 4
                for jc in range(4):
                    trk4[jc] = P.op("tensor", lambda e, jc=jc: e.matmul(
                        qbank(jc), wsum_neg[0:1, jc * 128:(jc + 1) * 128], mrow,
                        start=False, stop=True),
                        waits=[tm, t_wqsum], incs=[("sPE", 1)])
                    mrow_users.append(trk4[jc])
                xb_users[sl].append(trk4[3])
                wgt_users = [trk4[3]]
                for jc in range(4):
                    tqs = P.op("vector", lambda e, jc=jc: e.tensor_tensor(
                        out=qsc[:, jc % 2, :], in0=qbank(jc), in1=rbc[:], op=ALU.mult),
                        waits=[trk4[jc], trbc] + qsc_users[jc % 2], incs=[("sV", 1)])
                    qsc_users[jc % 2] = []
                    psq4[jc].append(tqs)
                    bb_ = ps6[:, 0:512] if jc % 2 == 0 else ps7[:, 0:512]
                    bu_ = ps6b_users if jc % 2 == 0 else ps7_users
                    tbm = P.op("tensor", lambda e, jc=jc, bb_=bb_: e.matmul(
                        bb_, bmat[:, jc // 2, :], qsc[:, jc % 2, :],
                        start=True, stop=True),
                        waits=[tqs, t_bm[jc // 2]] + bu_, incs=[("sPE", 1)])
                    if jc % 2 == 0:
                        ps6b_users = []
                    else:
                        ps7_users = []
                    qsc_users[jc % 2].append(tbm)
                    if jc < 2:
                        tev = P.op("scalar", lambda e, jc=jc, tb=tb, bb_=bb_: e.activation(
                            out=qeh[:, jc, tb * TB:(tb + 1) * TB], in_=bb_,
                            func=AF.Identity, bias=bbqcol[:, jc:jc + 1], scale=1.0),
                            waits=[tbm, t_bbq], incs=[("sSC", 1)])
                        (ps6b_users if jc % 2 == 0 else ps7_users).append(tev)
                        t_qh0 = tev
                    else:
                        tev = P.op("scalar", lambda e, jc=jc, bb_=bb_: e.activation(
                            out=stgb[:, jc % 2, 0, :], in_=bb_,
                            func=AF.Identity, bias=bbqcol[:, jc:jc + 1], scale=1.0),
                            waits=[tbm, t_bbq] + stgb_users[jc % 2], incs=[("sSC", 1)])
                        stgb_users[jc % 2] = []
                        (ps6b_users if jc % 2 == 0 else ps7_users).append(tev)
                        tsp = P.op("scalar", lambda e, jc=jc, tb=tb: e.dma_start(
                            out=qeffsp[(jc - 2) * 128:(jc - 1) * 128, tb * TB:(tb + 1) * TB],
                            in_=stgb[:, jc % 2, 0, :]), waits=[tev], incs=[("dSP", 16)])
                        stgb_users[jc % 2].append(tsp)
                        t_q_spill.append(tsp)
            ps6_users += ps6b_users
            ps6b_users = []
            t_sp_all_q = ("dSP", P.counts.get("dSP", 0), len(P.items) - 1)

        def plan_mr_load(tb, waits_extra=()):
            sl2 = tb % 2
            tlm = P.op("sync", lambda e, tb=tb, sl2=sl2: e.dma_start(
                out=mrl[:, sl2, :], in_=mr_rows[tb:tb + 1, :]),
                waits=[t_sp_all] + mrl_users[sl2] + list(waits_extra), incs=[("dMR", 16)])
            mrl_users[sl2] = []
            tlr = P.op("sync", lambda e, tb=tb, sl2=sl2: e.dma_start(
                out=rrl[:, sl2, :], in_=mr_rows[NTB + tb:NTB + tb + 1, :]),
                waits=[t_sp_all] + rrl_users[sl2], incs=[("dRR", 16)])
            rrl_users[sl2] = []
            return tlm, tlr

        mrl_users = [[], []]
        rrl_users = [[], []]
        t_mrl = [None] * (NTB + 1)
        t_rrl = [None] * (NTB + 1)
        if "k" in phases:
            t_wk = P.op("gpsimd", lambda e: e.dma_start(
                out=wgt[:].rearrange("p c j -> p (c j)"),
                in_=wkT.rearrange("(c p) j -> p c j", p=128)),
                waits=wgt_users, incs=[("dWG", 16)])
            t_wksum = wsum_mms(t_wk)
            t_sp_all = t_sp_all_q
            mrl_users[0] += mrow_users
            mrl_users[1] += srow_users
            rrl_users[0] += rrow_users
            rrl_users[1] += srow_users
            mrow_users, rrow_users, srow_users = [], [], []
            t_mrl[0], t_rrl[0] = plan_mr_load(0)
            t_xload[0] = P.op("sync", lambda e: e.dma_start(
                out=xbv[:, 0, :, :].rearrange("p c t -> p (c t)"),
                in_=xbf_ap[0:128, :]),
                waits=[t_sp_all] + xb_users[0], incs=[("dXA", 16)])
            xb_users[0] = []
            for tb in range(NTB):
                sl = tb % 2
                if tb + 1 < NTB:
                    t_xload[tb + 1] = P.op("sync", lambda e, tb=tb: e.dma_start(
                        out=xbv[:, (tb + 1) % 2, :, :].rearrange("p c t -> p (c t)"),
                        in_=xbf_ap[(tb + 1) * 128:(tb + 2) * 128, :]),
                        waits=xb_users[(tb + 1) % 2],
                        incs=[("dXA" if (tb + 1) % 2 == 0 else "dXB", 16)])
                    xb_users[(tb + 1) % 2] = []
                    t_mrl[tb + 1], t_rrl[tb + 1] = plan_mr_load(tb + 1)
                trk4 = [None] * 4
                for jc in range(4):
                    for c in range(NDC):
                        P.op("tensor", lambda e, c=c, jc=jc, sl=sl: e.matmul(
                            qbank(jc), wgt[:, c, jc * 128:(jc + 1) * 128], xbv[:, sl, c, :],
                            start=(c == 0), stop=False),
                            waits=(psq4[jc] + [t_xload[tb], t_wk]) if c == 0 else [])
                    psq4[jc] = []
                trb, trbc = bcast(t_rrl[tb], rrl[0:1, tb % 2, :], rbc[:], TB)
                rrl_users[tb % 2].append(trb)
                for jc in range(4):
                    trk4[jc] = P.op("tensor", lambda e, jc=jc, tb=tb: e.matmul(
                        qbank(jc), wsum_neg[0:1, jc * 128:(jc + 1) * 128], mrl[0:1, tb % 2, :],
                        start=False, stop=True),
                        waits=[t_mrl[tb], t_wksum], incs=[("sPE", 1)])
                    mrl_users[tb % 2].append(trk4[jc])
                xb_users[sl].append(trk4[3])
                wgt_users = [trk4[3]]
                for jc in range(4):
                    tks = P.op("vector", lambda e, jc=jc: e.tensor_tensor(
                        out=fsc[:, jc % 2, :], in0=qbank(jc), in1=rbc[:], op=ALU.mult),
                        waits=[trk4[jc], trbc] + fsc_users[jc % 2], incs=[("sV", 1)])
                    fsc_users[jc % 2] = []
                    psq4[jc].append(tks)
                    if jc < 2:
                        tev = P.op("scalar", lambda e, jc=jc, tb=tb: e.activation(
                            out=kh[:, jc, tb * TB:(tb + 1) * TB], in_=fsc[:, jc % 2, :],
                            func=AF.Identity, bias=bks[:, jc:jc + 1], scale=1.0),
                            waits=[tks, t_bk] + (sq_users if tb == 0 else []),
                            incs=[("sSC", 1)])
                        fsc_users[jc % 2].append(tev)
                        t_kh0 = tev
                    else:
                        tev = P.op("scalar", lambda e, jc=jc: e.activation(
                            out=stgb[:, jc % 2, 0, :], in_=fsc[:, jc % 2, :],
                            func=AF.Identity, bias=bks[:, jc:jc + 1], scale=1.0),
                            waits=[tks, t_bk] + stgb_users[jc % 2], incs=[("sSC", 1)])
                        stgb_users[jc % 2] = []
                        fsc_users[jc % 2].append(tev)
                        tsp = P.op("gpsimd", lambda e, jc=jc, tb=tb: e.dma_start(
                            out=ksp[(jc - 2) * 128:(jc - 1) * 128, tb * TB:(tb + 1) * TB],
                            in_=stgb[:, jc % 2, 0, :]), waits=[tev], incs=[("dSP", 16)])
                        stgb_users[jc % 2].append(tsp)
                        t_k_spill.append(tsp)

        if "v" in phases:
            t_wv = P.op("gpsimd", lambda e: e.dma_start(
                out=wgt[:].rearrange("p c j -> p (c j)"),
                in_=wvT.rearrange("(c p) j -> p c j", p=128)),
                waits=wgt_users, incs=[("dWG", 16)])
            t_wvsum = wsum_mms(t_wv)
            t_sp_all = t_sp_all_q
            t_mrl[0], t_rrl[0] = plan_mr_load(0)
            t_xload[0] = P.op("sync", lambda e: e.dma_start(
                out=xbv[:, 0, :, :].rearrange("p c t -> p (c t)"),
                in_=xbf_ap[0:128, :]),
                waits=[t_sp_all] + xb_users[0], incs=[("dXA", 16)])
            xb_users[0] = []
            for tb in range(NTB):
                sl = tb % 2
                if tb + 1 < NTB:
                    t_xload[tb + 1] = P.op("sync", lambda e, tb=tb: e.dma_start(
                        out=xbv[:, (tb + 1) % 2, :, :].rearrange("p c t -> p (c t)"),
                        in_=xbf_ap[(tb + 1) * 128:(tb + 2) * 128, :]),
                        waits=xb_users[(tb + 1) % 2],
                        incs=[("dXA" if (tb + 1) % 2 == 0 else "dXB", 16)])
                    xb_users[(tb + 1) % 2] = []
                    t_mrl[tb + 1], t_rrl[tb + 1] = plan_mr_load(tb + 1)
                trk4 = [None] * 4
                for tc in range(4):
                    for c in range(NDC):
                        P.op("tensor", lambda e, c=c, tc=tc, sl=sl: e.matmul(
                            qbank(tc), xbv[:, sl, c, tc * 128:(tc + 1) * 128], wgt[:, c, :],
                            start=(c == 0), stop=False),
                            waits=(psq4[tc] + [t_xload[tb], t_wv]) if c == 0 else [])
                    psq4[tc] = []
                ttr = None
                for tc in range(4):
                    ttr = P.op("tensor", lambda e, tc=tc, tb=tb: e.transpose(
                        ps6[:, tc:tc + 1], rrl[0:1, tb % 2, tc * 128:(tc + 1) * 128],
                        identf[0:1, 0:1]),
                        waits=([t_rrl[tb], t_identf] + ps6_users) if tc == 0 else [],
                        incs=[("sPE", 1)] if tc == 3 else [])
                ps6_users = []
                rrl_users[tb % 2].append(ttr)
                trc = P.op("vector", lambda e: e.tensor_copy(out=rcols[:], in_=ps6[:, 0:4]),
                           waits=[ttr] + rcol_users, incs=[("sV", 1)])
                rcol_users = []
                ps6_users.append(trc)
                for tc in range(4):
                    trk4[tc] = P.op("tensor", lambda e, tc=tc, tb=tb: e.matmul(
                        qbank(tc), mrl[0:1, tb % 2, tc * 128:(tc + 1) * 128], wsum_neg[:],
                        start=False, stop=True),
                        waits=[t_mrl[tb], t_wvsum], incs=[("sPE", 1)])
                    mrl_users[tb % 2].append(trk4[tc])
                xb_users[sl].append(trk4[3])
                wgt_users = [trk4[3]]
                for tc in range(4):
                    ti = tb * 4 + tc
                    tv = P.op("vector", lambda e, tc=tc: e.tensor_scalar(
                        out=stgb[:, tc % 2, :, :].rearrange("p a t -> p (a t)")[:, 0:512],
                        in0=qbank(tc), scalar1=rcols[:, tc:tc + 1], scalar2=None,
                        op0=ALU.mult),
                        waits=[trk4[tc], trc] + stgb_users[tc % 2], incs=[("sV", 1)])
                    stgb_users[tc % 2] = []
                    rcol_users.append(tv)
                    psq4[tc].append(tv)
                    for hh in range(2):
                        tsp = P.op("gpsimd", lambda e, ti=ti, tc=tc, hh=hh: e.dma_start(
                            out=vsp[hh * 128:hh * 128 + 128, ti * 256:(ti + 1) * 256],
                            in_=stgb[:, tc % 2, 0, hh * 256:(hh + 1) * 256]),
                            waits=[tv], incs=[("dSP", 16)])
                        stgb_users[tc % 2].append(tsp)
                        t_v_spill.append(tsp)
            psq_users = [psq4[0] + psq4[2], psq4[1] + psq4[3]]
            ps4_users = [psq4[2], psq4[3]]

        t_a2a_st = []
        exp_tok = {}
        prev_evac = list(ps4_users[0]) + list(ps4_users[1])
        ps4_users = [[], []]
        stg2_users = [[], []]
        last_exp = None
        last_avpe = None
        t_cc1 = None
        t_cc2 = None
        st_half = [[], []]
        n_half0 = HPC * 2 * (T // HTC) // 2 if T % (2 * HTC) == 0 else None
        if "p2" in phases:
            t_sp_all2 = ("dSP", P.counts.get("dSP", 0), len(P.items) - 1)
            srow_users += mrl_users[1] + rrl_users[1]
            mrow_users += mrl_users[0]
            rrow_users += rrl_users[0]
            mrl_users = [[], []]
            rrl_users = [[], []]
            t_wbbf = []
            for i4 in range(4):
                t_wbbf.append(P.op("gpsimd", lambda e, i4=i4: e.dma_start(
                    out=wbbf[:, i4 * 1024:(i4 + 1) * 1024],
                    in_=wbT[:, i4 * 1024:(i4 + 1) * 1024]), incs=[("dC3", 16)]))
            ar1_free = list(xb_users[0]) + list(xb_users[1])
            xb_users = [[], []]
            seedS = [pstat_users + [tv for tv in []], psq_users[0] + psq_users[1]]
            pstat_users = []
            psq_users = [[], []]
            for h in range(HPC):
                if h == 1:
                    tlq = P.op("sync", lambda e: e.dma_start(
                        out=qeh[:, :, :],
                        in_=qeffsp.rearrange("(a p) t -> p a t", p=128)),
                        waits=[t_sp_all2, last_avpe], incs=[("dP2", 16)])
                    tlk = P.op("sync", lambda e: e.dma_start(
                        out=kh[:, :, :],
                        in_=ksp.rearrange("(a p) t -> p a t", p=128)),
                        waits=[t_sp_all2, last_avpe], incs=[("dP2", 16)])
                tlv = P.op("sync", lambda e, h=h: e.dma_start(
                    out=vh[:, :, :].rearrange("p m j -> p (m j)"),
                    in_=vsp[h * 128:h * 128 + 128, :]),
                    waits=[t_sp_all2] + (wgt_users if h == 0 else [last_avpe]),
                    incs=[("dP2", 16)])
                res = ([t_qh0, t_kh0, tlv] if h == 0 else [tlq, tlk, tlv])
                qb_order = list(range(0, NQB, 2)) + list(range(1, NQB, 2))
                for qi, qb in enumerate(qb_order):
                    q0 = qb * TB
                    tAV_last = None
                    first_av = True
                    for g in range(NG + 1):
                        if g < NG:
                            w0 = list(res)
                            if g >= 2:
                                w0.append(exp_tok[(h, qb, g - 2)])
                            elif qi > 0:
                                w0.append(exp_tok[(h, qb_order[qi - 1], NG - 2 + g)])
                            elif h > 0:
                                w0.append(exp_tok[(h - 1, qb_order[-1], NG - 2 + g)])
                            else:
                                w0 += seedS[g % 2]
                            tSl = None
                            for mi in range(2):
                                mc = 2 * g + mi
                                for jj in range(2):
                                    tSl = P.op("tensor",
                                               lambda e, mc=mc, jj=jj, g=g, q0=q0: e.matmul(
                                                   ps0[:, (g % 2) * 1024 + (mc % 2) * 512:
                                                       (g % 2) * 1024 + (mc % 2) * 512 + 512],
                                                   kh[:, jj, mc * 128:(mc + 1) * 128],
                                                   qeh[:, jj, q0:q0 + TB],
                                                   start=(jj == 0), stop=(jj == 1)),
                                               waits=w0 if (mi == 0 and jj == 0) else [],
                                               incs=[("sPE", 1)] if (mi == 1 and jj == 1) else [])
                            wexp = [tSl]
                            if h == 0 and qb == 0 and g < 2:
                                wexp += ar1_free
                            texp = P.op("scalar", lambda e, g=g: e.activation(
                                out=pbv[:, 2 * g:2 * g + 2, :].rearrange("p a t -> p (a t)"),
                                in_=ps0[:, (g % 2) * 1024:(g % 2) * 1024 + 1024],
                                func=AF.Exp, scale=1.0),
                                waits=wexp, incs=[("sSC", 1)])
                            exp_tok[(h, qb, g)] = texp
                            last_exp = texp
                        ga = g - 1
                        if ga >= 0:
                            w1 = [exp_tok[(h, qb, ga)]]
                            if first_av:
                                w1 += prev_evac + ps6_users
                                prev_evac = []
                                ps6_users = []
                            for mi in range(2):
                                mc = 2 * ga + mi
                                st_ = bool(first_av and mi == 0)
                                sp_ = bool(ga == NG - 1 and mi == 1)
                                for ic in range(2):
                                    P.op("tensor",
                                         lambda e, mc=mc, ic=ic, st_=st_, sp_=sp_: e.matmul(
                                             ps4[:, ic * 512:(ic + 1) * 512],
                                             vh[:, mc, ic * 128:(ic + 1) * 128],
                                             pbv[:, mc, :],
                                             start=st_, stop=sp_),
                                         waits=w1 if (mi == 0 and ic == 0) else [])
                                tAV_last = P.op("tensor",
                                                lambda e, mc=mc, st_=st_, sp_=sp_: e.matmul(
                                                    ps6[0:1, 0:512], onesb[:], pbv[:, mc, :],
                                                    start=st_, stop=sp_),
                                                incs=[("sPE", 1)] if sp_ else [])
                            first_av = False
                    last_avpe = tAV_last
                    trs = P.op("vector", lambda e: e.tensor_copy(
                        out=srow0[0:1, 0:512], in_=ps6[0:1, 0:512]),
                        waits=[tAV_last] + srow_users, incs=[("sV", 1)])
                    srow_users = []
                    ps6_users.append(trs)
                    tln = P.op("scalar", lambda e: e.activation(
                        out=srow1[0:1, 0:512], in_=srow0[0:1, 0:512], func=AF.Ln),
                        waits=[trs], incs=[("sSC", 1)])
                    tinv = P.op("scalar", lambda e: e.activation(
                        out=srow0[0:1, 0:512], in_=srow1[0:1, 0:512], func=AF.Exp,
                        scale=-1.0), waits=[tln], incs=[("sSC", 1)])
                    srow_users = [tinv]
                    trb2, tibc = bcast(tinv, srow0[0:1, 0:512], rbc2[:], 512)
                    srow_users.append(trb2)
                    for ic in range(2):
                        tmul = P.op("vector", lambda e, ic=ic: e.tensor_tensor(
                            out=fsc[:, ic, :], in0=ps4[:, ic * 512:(ic + 1) * 512],
                            in1=rbc2[:], op=ALU.mult),
                            waits=[tibc, tAV_last] + fsc_users[ic], incs=[("sV", 1)])
                        fsc_users[ic] = []
                        prev_evac.append(tmul)
                        tbias = P.op("scalar", lambda e, ic=ic, h=h: e.activation(
                            out=stgb[:, ic, 1, :], in_=fsc[:, ic, :],
                            func=AF.Identity, bias=bvs[:, h * 2 + ic:h * 2 + ic + 1],
                            scale=1.0),
                            waits=[tmul, t_bv] + stg2_users[ic], incs=[("sSC", 1)])
                        stg2_users[ic] = []
                        fsc_users[ic].append(tbias)
                        for seg in range(q0 // HTC, (q0 + TB) // HTC):
                            tlo = seg * HTC
                            j = tlo // TC
                            hf = (tlo - j * TC) // HTC
                            tst = P.op("sync",
                                       lambda e, ic=ic, h=h, j=j, hf=hf, tlo=tlo, q0=q0:
                                       e.dma_start(
                                           out=a2a_in_aps[hf][
                                               j * JCORE + h * 256 + ic * 128:
                                               j * JCORE + h * 256 + (ic + 1) * 128, :],
                                           in_=stgb[:, ic, 1, tlo - q0:tlo - q0 + HTC]),
                                       waits=[tbias], incs=[("dSP", 16)])
                            t_a2a_st.append(tst)
                            st_half[hf].append(tst)
                            stg2_users[ic].append(tst)
                        if t_cc1 is None and len(st_half[0]) == n_half0:
                            def emit_cc1(e):
                                with allow_ici_all_to_all():
                                    return e.collective_compute(
                                        "AllToAll", ALU.bypass,
                                        replica_groups=[list(range(NCORE))],
                                        ins=[a2a_in1.ap().opt()],
                                        outs=[a2a_out1.ap().opt()])
                            t_cc1 = P.op("gpsimd", emit_cc1, waits=st_half[0],
                                         incs=[("sCC", 1)])

        if "cc" in phases:
            if t_cc1 is None:
                def emit_cc1b(e):
                    with allow_ici_all_to_all():
                        return e.collective_compute(
                            "AllToAll", ALU.bypass,
                            replica_groups=[list(range(NCORE))],
                            ins=[a2a_in1.ap().opt()],
                            outs=[a2a_out1.ap().opt()])
                t_cc1 = P.op("gpsimd", emit_cc1b, waits=st_half[0], incs=[("sCC", 1)])

            def emit_cc2(e):
                with allow_ici_all_to_all():
                    return e.collective_compute(
                        "AllToAll", ALU.bypass,
                        replica_groups=[list(range(NCORE))],
                        ins=[a2a_in2.ap().opt()],
                        outs=[a2a_out2.ap().opt()])
            t_cc2 = P.op("gpsimd", emit_cc2, waits=st_half[1], incs=[("sCC", 1)])

        if "p3" in phases:
            w_ar3 = [last_avpe] if last_avpe else []
            t_aT = P.op("gpsimd", lambda e: e.dma_start(
                out=aTb[:].rearrange("p c j -> p (c j)"),
                in_=aT.rearrange("(c p) j -> p c j", p=128)),
                waits=w_ar3, incs=[("dC3", 16)])
            t_b2d = P.op("gpsimd", lambda e: e.dma_start(out=b2dTb[:], in_=b2dT),
                         incs=[("dC3", 16)])
            t_wr = P.op("gpsimd", lambda e: e.dma_start(
                out=wrTb[:].rearrange("p c j -> p (c j)"),
                in_=wrT.rearrange("(c p) j -> p c j", p=128)), incs=[("dC3", 16)])
            t_c3all = ("dC3", P.counts.get("dC3", 0), len(P.items) - 1)
            xsv3 = AR3[:, 8192:12288].rearrange("p (c t) -> p c t", c=8)
            asv3 = AR3[:, 12288:16384].rearrange("p (c t) -> p c t", c=8)
            rbcB = stgb[:, 0, 0, :]
            rbc2B = stgb[:, 0, 1, :]
            SGR = min(128, HTC)
            NSUB = TB3 // SGR
            xs_users = list(w_ar3) + stg2_users[0] + stg2_users[1]
            as_users = list(w_ar3)
            stgb3_users = stg2_users[0] + stg2_users[1]
            x2t_users = [[], []]
            rl_users = []
            wem_users = []
            hs_users = []
            hwb_users = [[], []]
            hw_tok = [None] * NT3
            x2ln_tok = [None] * NT3
            stat_u = [last_exp] if last_exp else []
            pe_base_users = [[last_exp] if last_exp else [],
                             [last_exp] if last_exp else []]
            ar1p3 = [last_avpe] if last_avpe else []
            wb_slab_users = [[], []]
            t_wb = [None, None]
            ev1_users = []
            ost_users = [[], []]
            NSLAB = D // 256
            passA_tok = [None] * NT3

            x2spv = x2sp.rearrange("p (c t) -> p c t", t=TC)
            NCG = NDC // 8
            x_slv = x_sl.rearrange("p (c t) -> p c t", t=TC)

            def plan_passA(tb3):
                nonlocal xs_users, as_users, ar1p3, sq_users
                tk0 = tb3 * TB3
                tadds, tsaves = [], []
                for cg in range(NCG):
                    hf0 = tk0 // HTC
                    hf1 = (tk0 + TB3 - 1) // HTC
                    tx = P.op("gpsimd", lambda e, cg=cg, tk0=tk0: e.dma_start(
                        out=xsv3[:, 0:8, 0:TB3],
                        in_=x_slv[:, 8 * cg:8 * cg + 8, tk0:tk0 + TB3]),
                        waits=xs_users, incs=[("dXIA", 16)])
                    xs_users = []
                    tas = []
                    for hf in range(hf0, hf1 + 1):
                        c0 = max(tk0, hf * HTC)
                        c1_ = min(tk0 + TB3, (hf + 1) * HTC)
                        ta = P.op("sync", lambda e, cg=cg, hf=hf, c0=c0, c1_=c1_, tk0=tk0:
                                  e.dma_start(
                                      out=asv3[:, 0:8, c0 - tk0:c1_ - tk0],
                                      in_=a2a_out_aps[hf][cg * 1024:(cg + 1) * 1024,
                                                          c0 - hf * HTC:c1_ - hf * HTC]
                                      .rearrange("(c p) t -> p c t", p=128)),
                                  waits=[t_cc1 if hf == 0 else t_cc2] + as_users,
                                  incs=[("dAIA", 16)])
                        as_users = []
                        tas.append(ta)
                    tadd = P.op("vector", lambda e, cg=cg, tk0=tk0: e.tensor_tensor(
                        out=x2v[:, 8 * cg:8 * cg + 8, tk0:tk0 + TB3],
                        in0=xsv3[:, 0:8, 0:TB3],
                        in1=asv3[:, 0:8, 0:TB3], op=ALU.add),
                        waits=[tx] + tas + (ar1p3 if (tb3 == 0 and cg == 0) else []),
                        incs=[("sV", 1)])
                    if tb3 == 0 and cg == 0:
                        ar1p3 = []
                    xs_users.append(tadd)
                    as_users.append(tadd)
                    tadds.append(tadd)
                    tsv = P.op("sync", lambda e, cg=cg, tk0=tk0: e.dma_start(
                        out=x2spv[:, 8 * cg:8 * cg + 8, tk0:tk0 + TB3],
                        in_=x2v[:, 8 * cg:8 * cg + 8, tk0:tk0 + TB3]),
                        waits=[tadd], incs=[("dSP", 16)])
                    tsaves.append(tsv)
                tsq = None
                for c in range(NDC):
                    tsq = P.op("vector", lambda e, c=c, tk0=tk0: e.tensor_tensor(
                        out=sq3[:, c, :], in0=x2v[:, c, tk0:tk0 + TB3],
                        in1=x2v[:, c, tk0:tk0 + TB3], op=ALU.mult),
                        waits=([tadds[-1]] + sq_users +
                               ([last_avpe] if (tb3 == 0 and last_avpe) else []))
                        if c == 0 else [],
                        incs=[("sV", 1)] if c == NDC - 1 else [])
                sq_users = []
                passA_tok[tb3] = (tadds[-1], tsaves, tsq)

            plan_passA(0)
            for tb3 in range(NT3):
                tk0 = tb3 * TB3
                tlastadd, tsaves, tsq = passA_tok[tb3]
                w0 = stat_u + pstat_users + [tlastadd, t_ones]
                pstat_users = []
                stat_u = []
                for c in range(NDC):
                    P.op("tensor", lambda e, c=c, tk0=tk0: e.matmul(
                        ps0[0:1, 0:TB3], onesb[:], x2v[:, c, tk0:tk0 + TB3],
                        start=(c == 0), stop=(c == NDC - 1)),
                        waits=w0 if c == 0 else [])
                tstat = None
                for c in range(NDC):
                    tstat = P.op("tensor", lambda e, c=c: e.matmul(
                        ps0[0:1, 512:512 + TB3], onesb[:], sq3[:, c, :],
                        start=(c == 0), stop=(c == NDC - 1)),
                        waits=[tsq] if c == 0 else [],
                        incs=[("sPE", 1)] if c == NDC - 1 else [])
                mrow_users.append(tstat)
                tm, tr2 = row_chain(TB3)
                tmr = P.op("vector", lambda e: e.tensor_tensor(
                    out=srow1[0:1, 0:TB3], in0=mrow[0:1, 0:TB3], in1=rrow[0:1, 0:TB3],
                    op=ALU.mult), waits=[tr2, tm], incs=[("sV", 1)])
                mrow_users.append(tmr)
                rrow_users.append(tmr)
                srow_users.append(tmr)
                trb3, trbc3 = bcast(tr2, rrow[0:1, 0:TB3], rbcB[:, 0:TB3], TB3,
                                    extra=stgb3_users)
                rrow_users.append(trb3)
                trb4, tmbc3 = bcast(tmr, srow1[0:1, 0:TB3], rbc2B[:, 0:TB3], TB3,
                                    extra=stgb3_users)
                stgb3_users = []
                srow_users.append(trb4)
                tlast = None
                for c in range(NDC):
                    t1_ = P.op("vector", lambda e, c=c, tk0=tk0: e.tensor_tensor(
                        out=qsc[:, 0, 0:TB3], in0=x2v[:, c, tk0:tk0 + TB3],
                        in1=rbcB[:, 0:TB3], op=ALU.mult),
                        waits=([trbc3, tsq] + tsaves + qsc_users[0]) if c == 0 else [],
                        incs=[("sV", 1)])
                    qsc_users[0] = []
                    tlast = P.op("vector", lambda e, c=c, tk0=tk0: e.tensor_tensor(
                        out=x2v[:, c, tk0:tk0 + TB3], in0=qsc[:, 0, 0:TB3],
                        in1=rbc2B[:, 0:TB3], op=ALU.subtract),
                        waits=[t1_, tmbc3] if c == 0 else [t1_], incs=[("sV", 1)])
                    qsc_users[0] = [tlast]
                x2ln_tok[tb3] = tlast
                trt = None
                for c in range(NDC):
                    trt = P.op("tensor", lambda e, c=c, tk0=tk0: e.matmul(
                        ps6[0:8, 0:TB3], wrTb[:, c, :], x2v[:, c, tk0:tk0 + TB3],
                        start=(c == 0), stop=(c == NDC - 1)),
                        waits=([x2ln_tok[tb3], t_c3all] + ps6_users) if c == 0 else [],
                        incs=[("sPE", 1)] if c == NDC - 1 else [])
                ps6_users = []
                trl = P.op("vector", lambda e: e.tensor_copy(
                    out=rl[:, 0:TB3], in_=ps6[0:8, 0:TB3]),
                    waits=[trt] + rl_users, incs=[("sV", 1)])
                rl_users = []
                ps6_users.append(trl)
                th = None
                for c in range(NDC):
                    th = P.op("tensor", lambda e, c=c, tb3=tb3, tk0=tk0: e.matmul(
                        ps4[:, (tb3 % 2) * 512:(tb3 % 2) * 512 + TB3],
                        aTb[:, c, :], x2v[:, c, tk0:tk0 + TB3],
                        start=(c == 0), stop=(c == NDC - 1)),
                        waits=([x2ln_tok[tb3], t_c3all] + ps4_users[tb3 % 2] + prev_evac)
                        if c == 0 else [],
                        incs=[("sPE", 1)] if c == NDC - 1 else [])
                prev_evac = []
                ps4_users[tb3 % 2] = []
                ths = P.op("vector", lambda e, tb3=tb3: e.tensor_copy(
                    out=hs[:, 0:TB3], in_=ps4[:, (tb3 % 2) * 512:(tb3 % 2) * 512 + TB3]),
                    waits=[th] + hs_users, incs=[("sV", 1)])
                hs_users = []
                ps4_users[tb3 % 2].append(ths)
                wem_tok = None
                for tc in range(NTC3):
                    tt1 = P.op("tensor", lambda e, tc=tc: e.transpose(
                        ps7[:, tc * 8:(tc + 1) * 8], rl[:, tc * 128:(tc + 1) * 128],
                        identf[0:8, 0:8]),
                        waits=[trl, t_identf] + ps7_users,
                        incs=[("sPE", 1)])
                    ps7_users = []
                    rl_users.append(tt1)
                    tlt = P.op("vector", lambda e, tc=tc: e.tensor_copy(
                        out=lt8[:], in_=ps7[:, tc * 8:(tc + 1) * 8]),
                        waits=[tt1], incs=[("sV", 1)])
                    ps7_users.append(tlt)
                    tm1 = P.op("vector", lambda e: e.tensor_reduce(
                        out=c1[:], in_=lt8[:], axis=AX.X, op=ALU.max),
                        waits=[tlt], incs=[("sV", 1)])
                    tg1 = P.op("vector", lambda e: e.tensor_scalar(
                        out=ge1[:], in0=lt8[:], scalar1=c1[:, 0:1], scalar2=None,
                        op0=ALU.is_ge), waits=[tm1], incs=[("sV", 1)])
                    tl2 = P.op("vector", lambda e: e.scalar_tensor_tensor(
                        out=l28[:], in0=ge1[:], scalar=-1e9, in1=lt8[:],
                        op0=ALU.mult, op1=ALU.add), waits=[tg1], incs=[("sV", 1)])
                    tm2 = P.op("vector", lambda e: e.tensor_reduce(
                        out=c2[:], in_=l28[:], axis=AX.X, op=ALU.max),
                        waits=[tl2], incs=[("sV", 1)])
                    tg2 = P.op("vector", lambda e: e.tensor_scalar(
                        out=ge2[:], in0=l28[:], scalar1=c2[:, 0:1], scalar2=None,
                        op0=ALU.is_ge), waits=[tm2], incs=[("sV", 1)])
                    td = P.op("vector", lambda e: e.tensor_scalar(
                        out=c3[:], in0=c2[:], scalar1=c1[:, 0:1], scalar2=None,
                        op0=ALU.subtract), waits=[tm2], incs=[("sV", 1)])
                    tex = P.op("scalar", lambda e: e.activation(
                        out=c3[:], in_=c3[:], func=AF.Exp), waits=[td], incs=[("sSC", 1)])
                    tsum = P.op("vector", lambda e: e.tensor_scalar(
                        out=c4[:], in0=c3[:], scalar1=1.0, scalar2=None, op0=ALU.add),
                        waits=[tex], incs=[("sV", 1)])
                    tinv = P.op("vector", lambda e: e.reciprocal(out=c4[:], in_=c4[:]),
                                waits=[tsum], incs=[("sV", 1)])
                    tbw = P.op("vector", lambda e: e.tensor_scalar(
                        out=c3[:], in0=c3[:], scalar1=c4[:, 0:1], scalar2=None,
                        op0=ALU.mult), waits=[tinv], incs=[("sV", 1)])
                    tw1 = P.op("vector", lambda e: e.tensor_scalar(
                        out=wtm[:], in0=ge1[:], scalar1=c4[:, 0:1], scalar2=None,
                        op0=ALU.mult), waits=[tinv, tg1], incs=[("sV", 1)])
                    tw2 = P.op("vector", lambda e: e.scalar_tensor_tensor(
                        out=wtm[:], in0=ge2[:], scalar=c3[:, 0:1], in1=wtm[:],
                        op0=ALU.mult, op1=ALU.add),
                        waits=[tw1, tbw, tg2], incs=[("sV", 1)])
                    tt2 = P.op("tensor", lambda e, tc=tc: e.transpose(
                        ps6[0:8, tc * 128:(tc + 1) * 128], wtm[:], identf[:]),
                        waits=[tw2] + ps6_users, incs=[("sPE", 1)])
                    ps6_users = []
                    wem_tok = P.op("vector", lambda e, tc=tc: e.tensor_copy(
                        out=wem[:, tc * 128:(tc + 1) * 128],
                        in_=ps6[0:8, tc * 128:(tc + 1) * 128]),
                        waits=[tt2] + (wem_users if tc == 0 else []), incs=[("sV", 1)])
                    if tc == 0:
                        wem_users = []
                    ps6_users.append(wem_tok)
                tE = P.op("tensor", lambda e: e.matmul(
                    ps7[:, 0:TB3], e2b[:], wem[:, 0:TB3], start=True, stop=True),
                    waits=[wem_tok, t_e2] + ps7_users, incs=[("sPE", 1)])
                ps7_users = []
                wem_users.append(tE)
                hw_tok[tb3] = P.op("vector", lambda e, tb3=tb3: e.tensor_tensor(
                    out=hwb[:, tb3, :], in0=hs[:, 0:TB3], in1=ps7[:, 0:TB3], op=ALU.mult),
                    waits=[tE, ths] + hwb_users[tb3], incs=[("sV", 1)])
                hwb_users[tb3] = []
                hs_users.append(hw_tok[tb3])
                ps7_users.append(hw_tok[tb3])
                if tb3 + 1 < NT3:
                    plan_passA(tb3 + 1)
                t_wb[0] = P.op("scalar", lambda e, tb3=tb3: e.dma_start(
                    out=wbs[:, 0, :, :],
                    in_=wbbf[:, 0:256].rearrange("(c p) j -> p c j", p=128)),
                    waits=[t_c3all] + wb_slab_users[0] +
                          (xb_users[0] + xb_users[1] if tb3 == 0 else []),
                    incs=[("dWBA", 16)])
                if tb3 == 0:
                    xb_users = [[], []]
                wb_slab_users[0] = []
                for osl in range(NSLAB):
                    slp = osl % 2
                    if osl + 1 < NSLAB:
                        t_wb[(osl + 1) % 2] = P.op("scalar", lambda e, osl=osl: e.dma_start(
                            out=wbs[:, (osl + 1) % 2, :, :],
                            in_=wbbf[:, (osl + 1) * 256:(osl + 2) * 256].rearrange(
                                "(c p) j -> p c j", p=128)),
                            waits=wb_slab_users[(osl + 1) % 2],
                            incs=[("dWBA" if (osl + 1) % 2 == 0 else "dWBB", 16)])
                        wb_slab_users[(osl + 1) % 2] = []
                    for osub in range(2):
                        oc = osl * 2 + osub
                        par = oc % 2
                        bank = 2 + par
                        po = ps0[:, bank * 512:bank * 512 + TB3]
                        tx2 = P.op("scalar", lambda e, oc=oc, tk0=tk0, par=par: e.dma_start(
                            out=qsc[:, 1, 0:TB3] if par else stgb[:, 1, 0, 0:TB3],
                            in_=x2spv[:, oc, tk0:tk0 + TB3]),
                            waits=tsaves + x2t_users[par] +
                                  (stgb3_users if (tb3 == 0 and oc < 2) else []),
                            incs=[("dMR" if par == 0 else "dRR", 16)])
                        x2t_users[par] = []
                        tmain = None
                        for c in range(NDC):
                            tmain = P.op("tensor",
                                         lambda e, c=c, osub=osub, tk0=tk0, po=po, slp=slp:
                                         e.matmul(
                                             po, wbs[:, slp, c, osub * 128:(osub + 1) * 128],
                                             x2v[:, c, tk0:tk0 + TB3],
                                             start=(c == 0), stop=False),
                                         waits=(pe_base_users[par] +
                                                [t_wb[slp], x2ln_tok[tb3]]) if c == 0 else [])
                        pe_base_users[par] = []
                        tlora = P.op("tensor", lambda e, oc=oc, tb3=tb3, po=po: e.matmul(
                            po, b2dTb[:, oc * 128:(oc + 1) * 128], hwb[:, tb3, :],
                            start=False, stop=True),
                            waits=[hw_tok[tb3], t_c3all], incs=[("sPE", 1)])
                        hwb_users[tb3].append(tlora)
                        if osub == 1:
                            wb_slab_users[slp].append(tlora)
                        tb1 = P.op("scalar", lambda e, oc=oc, po=po: e.activation(
                            out=fsc[:, 1, 0:TB3], in_=po, func=AF.Identity,
                            bias=bbs[:, oc:oc + 1], scale=1.0),
                            waits=[tlora, t_bb] + fsc_users[1] + ev1_users,
                            incs=[("sSC", 1)])
                        fsc_users[1] = []
                        ev1_users = []
                        pe_base_users[par].append(tb1)
                        tadd2 = P.op("vector", lambda e, par=par: e.tensor_tensor(
                            out=ost[:, par, 0:TB3], in0=fsc[:, 1, 0:TB3],
                            in1=qsc[:, 1, 0:TB3] if par else stgb[:, 1, 0, 0:TB3],
                            op=ALU.add),
                            waits=[tb1, tx2] + ost_users[par],
                            incs=[("sV", 1)])
                        ost_users[par] = []
                        ev1_users.append(tadd2)
                        x2t_users[par].append(tadd2)
                        tst = P.op("sync", lambda e, oc=oc, tk0=tk0, par=par: e.dma_start(
                            out=out[oc * 128:(oc + 1) * 128, tk0:tk0 + TB3],
                            in_=ost[:, par, 0:TB3]), waits=[tadd2], incs=[("dOUT", 16)])
                        ost_users[par].append(tst)

        def _dma_tap(name, src_ap):
            if name in taps:
                P.op("sync", lambda e, s=src_ap, d=tap_aps[name]: e.dma_start(out=d, in_=s),
                     waits=[(s_, v_, len(P.items) - 1) for s_, v_ in P.counts.items()],
                     incs=[("dSP", 16)])
        _dma_tap("qeffsp", qeffsp)
        _dma_tap("ksp", ksp)
        _dma_tap("vsp", vsp)
        _dma_tap("mrows", mr_rows)
        _dma_tap("xbf", xbf_ap)
        _dma_tap("x2sp", x2sp)

        fin = [(s, v, len(P.items) - 1) for s, v in P.counts.items()]
        P.op("sync", None, waits=fin)

        with nc.Block() as block:
            @block.sync
            def _(e):
                P.emit("sync", e, sems)

            @block.tensor
            def _(e):
                P.emit("tensor", e, sems)

            @block.vector
            def _(e):
                P.emit("vector", e, sems)

            @block.scalar
            def _(e):
                P.emit("scalar", e, sems)

            @block.gpsimd
            def _(e):
                P.emit("gpsimd", e, sems)

    return nc


def _host_stage(inputs, T=8192):
    d0, kcx = _ga_consts()
    x = np.ascontiguousarray(inputs["L0_attn_x"][0].T)
    wq, wk, wv = inputs["L0_attn_Wq"], inputs["L0_attn_Wk"], inputs["L0_attn_Wv"]
    bq, bk, bv = inputs["L0_attn_bq"], inputs["L0_attn_bk"], inputs["L0_attn_bv"]
    bw = inputs["L0_attn_bias_weight"]
    wb, bb = inputs["L0_ffn_W_base"], inputs["L0_ffn_b_base"]
    wr, A, B = inputs["L0_ffn_W_router"], inputs["L0_ffn_A"], inputs["L0_ffn_B"]
    TC = T // NCORE
    d0kc = np.concatenate([d0, kcx[0], kcx[1], kcx[2]], axis=1)
    identF = np.eye(128, dtype=np.float32)
    e2 = np.zeros((8, 128), np.float32)
    for er in range(128):
        e2[er // 16, er] = 2.0
    wbT = np.ascontiguousarray(wb.T)
    aT = np.ascontiguousarray(np.asarray(A).reshape(128, D).T)
    b2dT = np.ascontiguousarray(np.asarray(B).transpose(1, 0, 2).reshape(D, 128).T)
    wrT = np.ascontiguousarray(np.asarray(wr).T)
    bbc = np.ascontiguousarray(np.asarray(bb).reshape(NDC, 128).T)
    NTB = T // TB
    x_pm = np.ascontiguousarray(
        x.reshape(NDC, 128, NTB, TB).transpose(2, 1, 0, 3).reshape(NTB * 128, NDC * TB))
    in_maps = []
    for c in range(NCORE):
        js = slice(c * JCORE, (c + 1) * JCORE)
        xs_ = x[:, c * TC:(c + 1) * TC]
        im = {
            "x_fm": x_pm,
            "x_sl": np.ascontiguousarray(
                xs_.reshape(NDC, 128, TC).transpose(1, 0, 2).reshape(128, NDC * TC)),
            "wqT": np.ascontiguousarray(np.asarray(wq).T[:, js]),
            "wkT": np.ascontiguousarray(np.asarray(wk).T[:, js]),
            "wvT": np.ascontiguousarray(np.asarray(wv).T[:, js]),
            "bqc": np.ascontiguousarray(np.asarray(bq)[js].reshape(4, 128).T),
            "bkc": np.ascontiguousarray(np.asarray(bk)[js].reshape(4, 128).T),
            "bvc": np.ascontiguousarray(np.asarray(bv)[js].reshape(4, 128).T),
            "bwt": np.ascontiguousarray(np.asarray(bw)[c * HPC:(c + 1) * HPC].reshape(1, HPC * 3)),
            "wbT": wbT,
            "bbc": bbc,
            "wrT": wrT,
            "aT": aT,
            "b2dT": b2dT,
            "d0kc": d0kc,
            "identF": identF,
            "e2c": e2,
        }
        in_maps.append({k: np.asarray(v, dtype=np.float32) for k, v in im.items()})
    return in_maps


def kernel(**inputs):
    T = 8192
    in_maps = _host_stage(inputs, T)
    nc = build(T)
    res = run_bass_kernel_spmd(nc, in_maps, core_ids=list(range(NCORE)))
    full = np.concatenate([res.results[c]["out"] for c in range(NCORE)], axis=1)
    return np.ascontiguousarray(full.T).reshape(1, T, D).astype(np.float32)


# version 11
# speedup vs baseline: 1.0419x; 1.0419x over previous
import numpy as np
from contextlib import ExitStack
import concourse.bass as bass
import concourse.mybir as mybir
from concourse.bass_utils import run_bass_kernel_spmd
from concourse.aot_env import allow_ici_all_to_all

F32 = mybir.dt.float32
BF16 = mybir.dt.bfloat16
AF = mybir.ActivationFunctionType
ALU = mybir.AluOpType
AX = mybir.AxisListType

D = 4096
NH, CD, NB = 16, 32, 8
NCORE = 8
JCORE = D // NCORE
HPC = 2
TB = 512
NDC = D // 128
SCALE = 1.0 / 16.0
EPS = 1e-5

ENGINES = ("sync", "tensor", "vector", "scalar", "gpsimd")


def _ga_consts():
    g = [1.0, 1.0, 1.0]
    dim, n = 3, 8
    masks = sorted(range(n), key=lambda m: (bin(m).count("1"), m))
    inv = {m: i for i, m in enumerate(masks)}
    C = np.zeros((n, n, n), dtype=np.float64)
    for i, a in enumerate(masks):
        for j, b in enumerate(masks):
            sw, at = 0, a >> 1
            while at:
                sw += bin(at & b).count("1")
                at >>= 1
            s = -1.0 if sw & 1 else 1.0
            for t in range(dim):
                if a >> t & 1 and b >> t & 1:
                    s *= g[t]
            C[i, j, inv[a ^ b]] = s
    grades = np.array([bin(m).count("1") for m in masks])
    rev = np.where(grades * (grades - 1) // 2 % 2 == 1, -1.0, 1.0)
    biv = C[:, :, [4, 5, 6]]
    d0 = np.zeros((128, 128))
    kcx = np.zeros((3, 128, 128))
    for c16 in range(16):
        for i in range(8):
            j = c16 * 8 + i
            d0[j, j] = rev[i] * SCALE
            for i2 in range(8):
                for cc in range(3):
                    kcx[cc, j, c16 * 8 + i2] = rev[i] * biv[i, i2, cc]
    return d0.astype(np.float32), kcx.astype(np.float32)


class Plan:
    def __init__(self):
        self.items = []
        self.counts = {}
        self.last_wait = {}

    def op(self, eng, emit, waits=(), incs=()):
        idx = len(self.items)
        wl = []
        for tok in waits:
            if tok is None:
                continue
            s, v, si = tok
            assert si < idx
            k = (eng, s)
            if v > self.last_wait.get(k, 0):
                self.last_wait[k] = v
                wl.append((s, v))
        incs = list(incs)
        assert len(incs) <= 1
        toks = []
        for s, amt in incs:
            self.counts[s] = self.counts.get(s, 0) + amt
            toks.append((s, self.counts[s], idx))
        self.items.append((eng, emit, wl, incs))
        return toks[0] if toks else None

    def emit(self, eng_name, e, sems):
        for engn, emit, wl, incs in self.items:
            if engn != eng_name:
                continue
            for s, v in wl:
                e.wait_ge(sems[s], v)
            if emit is None:
                continue
            instr = emit(e)
            for s, amt in incs:
                instr.then_inc(sems[s], amt)


def build(T=8192, phases=("p0", "q", "k", "v", "p2", "cc", "p3"), taps=(), tb3_override=None):
    NTB = T // TB
    TC = T // NCORE
    TB3 = tb3_override or min(TB, TC)
    NT3 = TC // TB3
    NQB = T // TB
    NMC = T // 128
    NG = NMC // 2
    NTC3 = TB3 // 128

    nc = bass.Bass("TRN2", target_bir_lowering=False, debug=False)

    x_fm = nc.dram_tensor("x_fm", [NTB * 128, NDC * TB], F32, kind="ExternalInput").ap()
    x_sl = nc.dram_tensor("x_sl", [128, NDC * TC], F32, kind="ExternalInput").ap()
    wqT = nc.dram_tensor("wqT", [D, JCORE], F32, kind="ExternalInput").ap()
    wkT = nc.dram_tensor("wkT", [D, JCORE], F32, kind="ExternalInput").ap()
    wvT = nc.dram_tensor("wvT", [D, JCORE], F32, kind="ExternalInput").ap()
    bqc = nc.dram_tensor("bqc", [128, 4], F32, kind="ExternalInput").ap()
    bkc = nc.dram_tensor("bkc", [128, 4], F32, kind="ExternalInput").ap()
    bvc = nc.dram_tensor("bvc", [128, 4], F32, kind="ExternalInput").ap()
    bwt = nc.dram_tensor("bwt", [1, HPC * 3], F32, kind="ExternalInput").ap()
    wbT = nc.dram_tensor("wbT", [D, D], F32, kind="ExternalInput").ap()
    bbc = nc.dram_tensor("bbc", [128, NDC], F32, kind="ExternalInput").ap()
    wrT = nc.dram_tensor("wrT", [D, 8], F32, kind="ExternalInput").ap()
    aT = nc.dram_tensor("aT", [D, 128], F32, kind="ExternalInput").ap()
    b2dT = nc.dram_tensor("b2dT", [128, D], F32, kind="ExternalInput").ap()
    d0kc = nc.dram_tensor("d0kc", [128, 4 * 128], F32, kind="ExternalInput").ap()
    identF = nc.dram_tensor("identF", [128, 128], F32, kind="ExternalInput").ap()
    e2c = nc.dram_tensor("e2c", [8, 128], F32, kind="ExternalInput").ap()
    out = nc.dram_tensor("out", [D, TC], F32, kind="ExternalOutput").ap()

    xbf = nc.dram_tensor("xbf", [NTB * 128, NDC * TB], BF16)
    mr_rows = nc.dram_tensor("mr_rows", [2 * NTB, TB], F32).ap()
    qeffsp = nc.dram_tensor("qeffsp", [2 * 128, T], BF16).ap()
    ksp = nc.dram_tensor("ksp", [2 * 128, T], BF16).ap()
    vsp = nc.dram_tensor("vsp", [2 * 128, NMC * 256], BF16).ap()
    HTC = TC // 2
    a2a_in1 = nc.dram_tensor("a2a_in1", [D, HTC], BF16)
    a2a_in2 = nc.dram_tensor("a2a_in2", [D, HTC], BF16)
    a2a_out1 = nc.dram_tensor("a2a_out1", [D, HTC], BF16)
    a2a_out2 = nc.dram_tensor("a2a_out2", [D, HTC], BF16)
    x2sp = nc.dram_tensor("x2sp", [128, NDC * TC], BF16).ap()
    wbbf = nc.dram_tensor("wbbf", [D, D], BF16).ap()
    xbf_ap = xbf.ap()
    a2a_in_aps = [a2a_in1.ap(), a2a_in2.ap()]
    a2a_out_aps = [a2a_out1.ap(), a2a_out2.ap()]

    tap_aps = {}
    for t_ in taps:
        shp, dt_ = {
            "qeffsp": ([2 * 128, T], BF16), "ksp": ([2 * 128, T], BF16),
            "vsp": ([2 * 128, NMC * 256], BF16), "mrows": ([2 * NTB, TB], F32),
            "xbf": ([NTB * 128, NDC * TB], BF16), "x2sp": ([128, NDC * TC], BF16),
        }[t_]
        tap_aps[t_] = nc.dram_tensor("tap_" + t_, shp, dt_, kind="ExternalOutput").ap()

    P = Plan()
    sem_names = ["dXA", "dXB", "dWG", "dMR", "dRR", "dWBA", "dWBB",
                 "dXIA", "dXIB", "dAIA", "dAIB", "dP2", "dCST", "dC3",
                 "dSP", "dOUT", "sCC", "sPE", "sV", "sSC", "sGP"]

    with ExitStack() as ctx:
        sems = {n: ctx.enter_context(nc.semaphore(n)) for n in sem_names}

        def sb(name, shape, dt):
            return ctx.enter_context(nc.sbuf_tensor(name, shape, dt))

        def psum(name, shape, dt):
            return ctx.enter_context(nc.psum_tensor(name, shape, dt))

        AR1 = sb("AR1", [128, 32768], BF16)
        AR2 = sb("AR2", [128, 16384], BF16)
        AR3 = sb("AR3", [128, 16384], BF16)
        AR4 = sb("AR4", [128, 16384], BF16)
        xbv = AR1[:].rearrange("p (s c t) -> p s c t", s=2, c=NDC)
        pbv = AR1[:, 0:NMC * TB].rearrange("p (m t) -> p m t", t=TB)
        x2v = AR1[:, 0:NDC * TC].rearrange("p (c t) -> p c t", t=TC)
        wgt = AR2[:].rearrange("p (c j) -> p c j", c=NDC)
        vh = AR2[:, 0:NMC * 256].rearrange("p (m j) -> p m j", j=256)
        wbs = AR2[:].rearrange("p (s c j) -> p s c j", s=2, c=NDC)
        sq1 = AR3[:].rearrange("p (c t) -> p c t", c=NDC)
        kh = AR3[:, 0:2 * T].rearrange("p (a t) -> p a t", a=2)
        aTb = AR3[:, 0:NDC * 128].rearrange("p (c j) -> p c j", c=NDC)
        b2dTb = AR3[:, NDC * 128:NDC * 128 + D]
        qeh = AR4[:, 0:2 * T].rearrange("p (a t) -> p a t", a=2)
        sq3 = AR4[:, 0:NDC * TB3].rearrange("p (c t) -> p c t", c=NDC)

        mrl = sb("mrl", [1, 2, TB], F32)
        rrl = sb("rrl", [1, 2, TB], F32)
        mrow = mrl[0:1, 0, :]
        rrow = rrl[0:1, 0, :]
        srow0 = mrl[0:1, 1, :]
        srow1 = rrl[0:1, 1, :]
        wsum_neg = sb("wsum_neg", [1, JCORE], F32)
        rbc = sb("rbc", [128, TB], F32)
        rbc2 = sb("rbc2", [128, TB], F32)
        rcols = sb("rcols", [128, 4], F32)
        qsc = sb("qsc", [128, 2, TB], BF16)
        fsc = sb("fsc", [128, 2, TB], F32)
        stgb = sb("stgb", [128, 2, 2, TB], BF16)
        ost = sb("ost", [128, 2, TB3], F32)
        identf = sb("identf", [128, 128], F32)
        d0kcs = sb("d0kcs", [128, 4, 128], BF16)
        bmat = sb("bmat", [128, 2, 128], BF16)
        bmatf = sb("bmatf", [128, 2, 128], F32)
        bwcol = sb("bwcol", [128, 6], F32)
        bbqcol = sb("bbqcol", [128, 4], F32)
        bqs = sb("bqs", [128, 4], F32)
        bks = sb("bks", [128, 4], F32)
        bvs = sb("bvs", [128, 4], F32)
        bbs = sb("bbs", [128, NDC], F32)
        bws = sb("bws", [1, HPC * 3], F32)
        e2b = sb("e2b", [8, 128], BF16)
        wrTb = sb("wrTb", [128, NDC, 8], BF16)
        hs = sb("hs", [128, TB3], F32)
        hwb = sb("hwb", [128, 2, TB3], BF16)
        rl = sb("rl", [8, TB3], F32)
        wem = sb("wem", [8, TB3], BF16)
        lt8 = sb("lt8", [128, 8], F32)
        l28 = sb("l28", [128, 8], F32)
        ge1 = sb("ge1", [128, 8], F32)
        ge2 = sb("ge2", [128, 8], F32)
        wtm = sb("wtm", [128, 8], F32)
        c1 = sb("c1", [128, 1], F32)
        c2 = sb("c2", [128, 1], F32)
        c3 = sb("c3", [128, 1], F32)
        c4 = sb("c4", [128, 1], F32)
        onesb = sb("onesb", [128, 1], BF16)
        onesf = sb("onesf", [1, 128], F32)
        epst = sb("epst", [1, 1], F32)

        ps0 = psum("ps0", [128, 2048], F32)
        ps4 = psum("ps4", [128, 1024], F32)
        ps6 = psum("ps6", [128, 512], F32)
        ps7 = psum("ps7", [128, 512], F32)

        t_identf = P.op("sync", lambda e: e.dma_start(out=identf[:], in_=identF), incs=[("dCST", 16)])
        t_d0kc = P.op("gpsimd", lambda e: e.dma_start(
            out=d0kcs[:].rearrange("p a b -> p (a b)"), in_=d0kc), incs=[("dCST", 16)])
        t_bq = P.op("sync", lambda e: e.dma_start(out=bqs[:], in_=bqc), incs=[("dCST", 16)])
        t_bk = P.op("sync", lambda e: e.dma_start(out=bks[:], in_=bkc), incs=[("dCST", 16)])
        t_bv = P.op("sync", lambda e: e.dma_start(out=bvs[:], in_=bvc), incs=[("dCST", 16)])
        t_bb = P.op("sync", lambda e: e.dma_start(out=bbs[:], in_=bbc), incs=[("dCST", 16)])
        t_bw = P.op("sync", lambda e: e.dma_start(out=bws[:], in_=bwt), incs=[("dCST", 16)])
        t_e2 = P.op("gpsimd", lambda e: e.dma_start(out=e2b[:], in_=e2c), incs=[("dCST", 16)])
        t_cst_all = ("dCST", P.counts.get("dCST", 0), len(P.items) - 1)
        t_identf = t_d0kc = t_bq = t_bk = t_bv = t_bb = t_bw = t_e2 = t_cst_all
        t_ones = P.op("vector", lambda e: e.memset(onesb[:], 1.0), incs=[("sV", 1)])
        t_onesf = P.op("vector", lambda e: e.memset(onesf[:], 1.0), incs=[("sV", 1)])
        t_eps = P.op("vector", lambda e: e.memset(epst[:], EPS), incs=[("sV", 1)])

        t_bm = [None, None]
        t_bbq = None
        if "p0" in phases:
            t_bwc = P.op("tensor", lambda e: e.matmul(
                ps6[:, 0:6], onesf[:], bws[0:1, 0:6], start=True, stop=True),
                waits=[t_onesf, t_bw], incs=[("sPE", 1)])
            t_bwcp = P.op("vector", lambda e: e.tensor_copy(out=bwcol[:], in_=ps6[:, 0:6]),
                          waits=[t_bwc], incs=[("sV", 1)])
            for h in range(HPC):
                tt0 = P.op("vector", lambda e, h=h: e.scalar_tensor_tensor(
                    out=bmatf[:, h, :], in0=d0kcs[:, 1, :], scalar=bwcol[:, h * 3:h * 3 + 1],
                    in1=d0kcs[:, 0, :], op0=ALU.mult, op1=ALU.add),
                    waits=[t_bwcp, t_d0kc], incs=[("sV", 1)])
                tt1 = P.op("vector", lambda e, h=h: e.scalar_tensor_tensor(
                    out=bmatf[:, h, :], in0=d0kcs[:, 2, :], scalar=bwcol[:, h * 3 + 1:h * 3 + 2],
                    in1=bmatf[:, h, :], op0=ALU.mult, op1=ALU.add),
                    waits=[tt0], incs=[("sV", 1)])
                tt2 = P.op("vector", lambda e, h=h: e.scalar_tensor_tensor(
                    out=bmatf[:, h, :], in0=d0kcs[:, 3, :], scalar=bwcol[:, h * 3 + 2:h * 3 + 3],
                    in1=bmatf[:, h, :], op0=ALU.mult, op1=ALU.add),
                    waits=[tt1], incs=[("sV", 1)])
                t_bm[h] = P.op("vector", lambda e, h=h: e.tensor_copy(
                    out=bmat[:, h, :], in_=bmatf[:, h, :]), waits=[tt2], incs=[("sV", 1)])
            t_bbqm = []
            for h in range(HPC):
                t_bbqm.append(P.op("tensor", lambda e, h=h: e.matmul(
                    ps7[:, 2 * h:2 * h + 2], bmatf[:, h, :], bqs[:, 2 * h:2 * h + 2],
                    start=True, stop=True),
                    waits=[t_bm[h], t_bq], incs=[("sPE", 1)]))
            t_bbq = P.op("vector", lambda e: e.tensor_copy(out=bbqcol[:], in_=ps7[:, 0:4]),
                         waits=[t_bbqm[-1]], incs=[("sV", 1)])

        t_mr_save = [None] * (2 * NTB)
        t_xbf_save = [None] * NTB
        t_q_spill, t_k_spill, t_v_spill = [], [], []
        xb_users = [[], []]
        qsc_users = [[], []]
        fsc_users = [[], []]
        stgb_users = [[], []]
        psq_users = [[], []]
        pstat_users = []
        ps4_users = [[], []]
        ps6_users = [t_bwcp] and []
        ps6_users = []
        ps7_users = [t_bbq] if t_bbq else []
        sq_users = []
        mrow_users = []
        rrow_users = []
        srow_users = []
        wgt_users = []
        rcol_users = []

        if "p0" in phases:
            ps6_users.append(t_bwcp)

        def row_chain(tb3_len, save_idx=None):
            nonlocal pstat_users, mrow_users, rrow_users, srow_users, ps7_users
            L = tb3_len
            tm = P.op("vector", lambda e, L=L: e.tensor_scalar(
                out=mrow[0:1, 0:L], in0=ps0[0:1, 0:L], scalar1=1.0 / D, scalar2=None,
                op0=ALU.mult), waits=mrow_users, incs=[("sV", 1)])
            mrow_users = []
            tmsq = P.op("vector", lambda e, L=L: e.tensor_tensor(
                out=srow0[0:1, 0:L], in0=mrow[0:1, 0:L], in1=mrow[0:1, 0:L], op=ALU.mult),
                waits=[tm] + srow_users, incs=[("sV", 1)])
            srow_users = []
            tvar = P.op("vector", lambda e, L=L: e.scalar_tensor_tensor(
                out=srow1[0:1, 0:L], in0=ps0[0:1, 512:512 + L], scalar=1.0 / D,
                in1=srow0[0:1, 0:L], op0=ALU.mult, op1=ALU.subtract),
                waits=[tmsq], incs=[("sV", 1)])
            pstat_users = [tm, tvar]
            tln = P.op("scalar", lambda e, L=L: e.activation(
                out=srow0[0:1, 0:L], in_=srow1[0:1, 0:L], func=AF.Ln, bias=epst[0:1, 0:1],
                scale=1.0), waits=[tvar, t_eps], incs=[("sSC", 1)])
            tr = P.op("scalar", lambda e, L=L: e.activation(
                out=rrow[0:1, 0:L], in_=srow0[0:1, 0:L], func=AF.Exp, scale=-0.5),
                waits=[tln] + rrow_users, incs=[("sSC", 1)])
            rrow_users = []
            srow_users = [tr]
            return tm, tr

        def bcast(row_tok, row_ap, dst_ap, L, extra=()):
            nonlocal ps7_users
            trb = P.op("tensor", lambda e, row_ap=row_ap, L=L: e.matmul(
                ps7[:, 0:L], onesf[:], row_ap, start=True, stop=True),
                waits=[row_tok, t_onesf] + ps7_users, incs=[("sPE", 1)])
            ps7_users = []
            tc_ = P.op("vector", lambda e, dst_ap=dst_ap, L=L: e.tensor_copy(
                out=dst_ap, in_=ps7[:, 0:L]), waits=[trb] + list(extra),
                incs=[("sV", 1)])
            ps7_users.append(tc_)
            return trb, tc_

        def wsum_mms(t_wload):
            nonlocal ps6_users
            tk = None
            for c in range(NDC):
                tk = P.op("tensor", lambda e, c=c: e.matmul(
                    ps6[0:1, 0:512], onesb[:], wgt[:, c, :],
                    start=(c == 0), stop=(c == NDC - 1)),
                    waits=([t_wload, t_ones] + ps6_users) if c == 0 else [],
                    incs=[("sPE", 1)] if c == NDC - 1 else [])
            ps6_users = []
            tneg = P.op("vector", lambda e: e.tensor_scalar(
                out=wsum_neg[:], in0=ps6[0:1, 0:512], scalar1=-1.0, scalar2=None,
                op0=ALU.mult), waits=[tk], incs=[("sV", 1)])
            ps6_users.append(tneg)
            return tneg

        t_xload = [None] * (NTB + 1)

        QB = None
        def qbank(jc):
            return (ps0[:, 1024:1536], ps0[:, 1536:2048],
                    ps4[:, 0:512], ps4[:, 512:1024])[jc]
        psq4 = [[], [], [], []]
        ps6b_users = []
        if "q" in phases:
            t_wq = P.op("gpsimd", lambda e: e.dma_start(
                out=wgt[:].rearrange("p c j -> p (c j)"),
                in_=wqT.rearrange("(c p) j -> p c j", p=128)), incs=[("dWG", 16)])
            t_wqsum = wsum_mms(t_wq)
            t_xload[0] = P.op("gpsimd", lambda e: e.dma_start(
                out=xbv[:, 0, :, :].rearrange("p c t -> p (c t)"),
                in_=x_fm[0:128, :]), incs=[("dXA", 16)])
            for tb in range(NTB):
                sl = tb % 2
                if tb + 1 < NTB:
                    t_xload[tb + 1] = P.op("gpsimd", lambda e, tb=tb: e.dma_start(
                        out=xbv[:, (tb + 1) % 2, :, :].rearrange("p c t -> p (c t)"),
                        in_=x_fm[(tb + 1) * 128:(tb + 2) * 128, :]),
                        waits=xb_users[(tb + 1) % 2],
                        incs=[("dXA" if (tb + 1) % 2 == 0 else "dXB", 16)])
                    xb_users[(tb + 1) % 2] = []
                t_xbf_save[tb] = P.op("sync", lambda e, tb=tb, sl=sl: e.dma_start(
                    out=xbf_ap[tb * 128:(tb + 1) * 128, :],
                    in_=xbv[:, sl, :, :].rearrange("p c t -> p (c t)")),
                    waits=[t_xload[tb]], incs=[("dSP", 16)])
                xb_users[sl].append(t_xbf_save[tb])
                tsq = None
                for c in range(NDC):
                    tsq = P.op("vector", lambda e, c=c, sl=sl: e.tensor_tensor(
                        out=sq1[:, c, :], in0=xbv[:, sl, c, :], in1=xbv[:, sl, c, :],
                        op=ALU.mult),
                        waits=([t_xload[tb]] + sq_users) if c == 0 else [],
                        incs=[("sV", 1)] if c == NDC - 1 else [])
                sq_users = []
                xb_users[sl].append(tsq)
                w0 = pstat_users + [t_xload[tb], t_ones]
                for c in range(NDC):
                    P.op("tensor", lambda e, c=c, sl=sl: e.matmul(
                        ps0[0:1, 0:512], onesb[:], xbv[:, sl, c, :],
                        start=(c == 0), stop=(c == NDC - 1)),
                        waits=w0 if c == 0 else [])
                pstat_users = []
                tmains = [None] * 4
                for c in range(NDC):
                    tk = P.op("tensor", lambda e, c=c, sl=sl: e.matmul(
                        qbank(0), wgt[:, c, 0:128], xbv[:, sl, c, :],
                        start=(c == 0), stop=False),
                        waits=(psq4[0] + [t_wq]) if c == 0 else [],
                        incs=[("sPE", 1)] if c == NDC - 1 else [])
                psq4[0] = []
                tmains[0] = tk
                tstat = None
                for c in range(NDC):
                    tstat = P.op("tensor", lambda e, c=c: e.matmul(
                        ps0[0:1, 512:1024], onesb[:], sq1[:, c, :],
                        start=(c == 0), stop=(c == NDC - 1)),
                        waits=[tsq] if c == 0 else [],
                        incs=[("sPE", 1)] if c == NDC - 1 else [])
                mrow_users.append(tstat)
                sq_users.append(tstat)
                for jc in range(1, 4):
                    for c in range(NDC):
                        tk = P.op("tensor", lambda e, c=c, jc=jc, sl=sl: e.matmul(
                            qbank(jc), wgt[:, c, jc * 128:(jc + 1) * 128], xbv[:, sl, c, :],
                            start=(c == 0), stop=False),
                            waits=psq4[jc] if c == 0 else [],
                            incs=[("sPE", 1)] if c == NDC - 1 else [])
                    psq4[jc] = []
                    tmains[jc] = tk
                tm, tr = row_chain(TB)
                t_mr_save[tb] = P.op("sync", lambda e, tb=tb: e.dma_start(
                    out=mr_rows[tb:tb + 1, :], in_=mrow), waits=[tm], incs=[("dSP", 16)])
                t_mr_save[NTB + tb] = P.op("sync", lambda e, tb=tb: e.dma_start(
                    out=mr_rows[NTB + tb:NTB + tb + 1, :], in_=rrow),
                    waits=[tr], incs=[("dSP", 16)])
                mrow_users.append(t_mr_save[tb])
                rrow_users.append(t_mr_save[NTB + tb])
                trb, trbc = bcast(tr, rrow, rbc[:], TB)
                rrow_users.append(trb)
                trk4 = [None] * 4
                for jc in range(4):
                    trk4[jc] = P.op("tensor", lambda e, jc=jc: e.matmul(
                        qbank(jc), wsum_neg[0:1, jc * 128:(jc + 1) * 128], mrow,
                        start=False, stop=True),
                        waits=[tm, t_wqsum], incs=[("sPE", 1)])
                    mrow_users.append(trk4[jc])
                xb_users[sl].append(trk4[3])
                wgt_users = [trk4[3]]
                for jc in range(4):
                    tqs = P.op("vector", lambda e, jc=jc: e.tensor_tensor(
                        out=qsc[:, jc % 2, :], in0=qbank(jc), in1=rbc[:], op=ALU.mult),
                        waits=[trk4[jc], trbc] + qsc_users[jc % 2], incs=[("sV", 1)])
                    qsc_users[jc % 2] = []
                    psq4[jc].append(tqs)
                    bb_ = ps6[:, 0:512] if jc % 2 == 0 else ps7[:, 0:512]
                    bu_ = ps6b_users if jc % 2 == 0 else ps7_users
                    tbm = P.op("tensor", lambda e, jc=jc, bb_=bb_: e.matmul(
                        bb_, bmat[:, jc // 2, :], qsc[:, jc % 2, :],
                        start=True, stop=True),
                        waits=[tqs, t_bm[jc // 2]] + bu_, incs=[("sPE", 1)])
                    if jc % 2 == 0:
                        ps6b_users = []
                    else:
                        ps7_users = []
                    qsc_users[jc % 2].append(tbm)
                    if jc < 2:
                        tev = P.op("scalar", lambda e, jc=jc, tb=tb, bb_=bb_: e.activation(
                            out=qeh[:, jc, tb * TB:(tb + 1) * TB], in_=bb_,
                            func=AF.Identity, bias=bbqcol[:, jc:jc + 1], scale=1.0),
                            waits=[tbm, t_bbq], incs=[("sSC", 1)])
                        (ps6b_users if jc % 2 == 0 else ps7_users).append(tev)
                        t_qh0 = tev
                    else:
                        tev = P.op("scalar", lambda e, jc=jc, bb_=bb_: e.activation(
                            out=stgb[:, jc % 2, 0, :], in_=bb_,
                            func=AF.Identity, bias=bbqcol[:, jc:jc + 1], scale=1.0),
                            waits=[tbm, t_bbq] + stgb_users[jc % 2], incs=[("sSC", 1)])
                        stgb_users[jc % 2] = []
                        (ps6b_users if jc % 2 == 0 else ps7_users).append(tev)
                        tsp = P.op("scalar", lambda e, jc=jc, tb=tb: e.dma_start(
                            out=qeffsp[(jc - 2) * 128:(jc - 1) * 128, tb * TB:(tb + 1) * TB],
                            in_=stgb[:, jc % 2, 0, :]), waits=[tev], incs=[("dSP", 16)])
                        stgb_users[jc % 2].append(tsp)
                        t_q_spill.append(tsp)
            ps6_users += ps6b_users
            ps6b_users = []
            t_sp_all_q = ("dSP", P.counts.get("dSP", 0), len(P.items) - 1)

        def plan_mr_load(tb, waits_extra=()):
            sl2 = tb % 2
            tlm = P.op("sync", lambda e, tb=tb, sl2=sl2: e.dma_start(
                out=mrl[:, sl2, :], in_=mr_rows[tb:tb + 1, :]),
                waits=[t_sp_all] + mrl_users[sl2] + list(waits_extra), incs=[("dMR", 16)])
            mrl_users[sl2] = []
            tlr = P.op("sync", lambda e, tb=tb, sl2=sl2: e.dma_start(
                out=rrl[:, sl2, :], in_=mr_rows[NTB + tb:NTB + tb + 1, :]),
                waits=[t_sp_all] + rrl_users[sl2], incs=[("dRR", 16)])
            rrl_users[sl2] = []
            return tlm, tlr

        mrl_users = [[], []]
        rrl_users = [[], []]
        t_mrl = [None] * (NTB + 1)
        t_rrl = [None] * (NTB + 1)
        if "k" in phases:
            t_wk = P.op("gpsimd", lambda e: e.dma_start(
                out=wgt[:].rearrange("p c j -> p (c j)"),
                in_=wkT.rearrange("(c p) j -> p c j", p=128)),
                waits=wgt_users, incs=[("dWG", 16)])
            t_wksum = wsum_mms(t_wk)
            t_sp_all = t_sp_all_q
            mrl_users[0] += mrow_users
            mrl_users[1] += srow_users
            rrl_users[0] += rrow_users
            rrl_users[1] += srow_users
            mrow_users, rrow_users, srow_users = [], [], []
            t_mrl[0], t_rrl[0] = plan_mr_load(0)
            t_xload[0] = P.op("sync", lambda e: e.dma_start(
                out=xbv[:, 0, :, :].rearrange("p c t -> p (c t)"),
                in_=xbf_ap[0:128, :]),
                waits=[t_sp_all] + xb_users[0], incs=[("dXA", 16)])
            xb_users[0] = []
            for tb in range(NTB):
                sl = tb % 2
                if tb + 1 < NTB:
                    t_xload[tb + 1] = P.op("sync", lambda e, tb=tb: e.dma_start(
                        out=xbv[:, (tb + 1) % 2, :, :].rearrange("p c t -> p (c t)"),
                        in_=xbf_ap[(tb + 1) * 128:(tb + 2) * 128, :]),
                        waits=xb_users[(tb + 1) % 2],
                        incs=[("dXA" if (tb + 1) % 2 == 0 else "dXB", 16)])
                    xb_users[(tb + 1) % 2] = []
                    t_mrl[tb + 1], t_rrl[tb + 1] = plan_mr_load(tb + 1)
                trk4 = [None] * 4
                for jc in range(4):
                    for c in range(NDC):
                        P.op("tensor", lambda e, c=c, jc=jc, sl=sl: e.matmul(
                            qbank(jc), wgt[:, c, jc * 128:(jc + 1) * 128], xbv[:, sl, c, :],
                            start=(c == 0), stop=False),
                            waits=(psq4[jc] + [t_xload[tb], t_wk]) if c == 0 else [])
                    psq4[jc] = []
                trb, trbc = bcast(t_rrl[tb], rrl[0:1, tb % 2, :], rbc[:], TB)
                rrl_users[tb % 2].append(trb)
                for jc in range(4):
                    trk4[jc] = P.op("tensor", lambda e, jc=jc, tb=tb: e.matmul(
                        qbank(jc), wsum_neg[0:1, jc * 128:(jc + 1) * 128], mrl[0:1, tb % 2, :],
                        start=False, stop=True),
                        waits=[t_mrl[tb], t_wksum], incs=[("sPE", 1)])
                    mrl_users[tb % 2].append(trk4[jc])
                xb_users[sl].append(trk4[3])
                wgt_users = [trk4[3]]
                for jc in range(4):
                    tks = P.op("vector", lambda e, jc=jc: e.tensor_tensor(
                        out=fsc[:, jc % 2, :], in0=qbank(jc), in1=rbc[:], op=ALU.mult),
                        waits=[trk4[jc], trbc] + fsc_users[jc % 2], incs=[("sV", 1)])
                    fsc_users[jc % 2] = []
                    psq4[jc].append(tks)
                    if jc < 2:
                        tev = P.op("scalar", lambda e, jc=jc, tb=tb: e.activation(
                            out=kh[:, jc, tb * TB:(tb + 1) * TB], in_=fsc[:, jc % 2, :],
                            func=AF.Identity, bias=bks[:, jc:jc + 1], scale=1.0),
                            waits=[tks, t_bk] + (sq_users if tb == 0 else []),
                            incs=[("sSC", 1)])
                        fsc_users[jc % 2].append(tev)
                        t_kh0 = tev
                    else:
                        tev = P.op("scalar", lambda e, jc=jc: e.activation(
                            out=stgb[:, jc % 2, 0, :], in_=fsc[:, jc % 2, :],
                            func=AF.Identity, bias=bks[:, jc:jc + 1], scale=1.0),
                            waits=[tks, t_bk] + stgb_users[jc % 2], incs=[("sSC", 1)])
                        stgb_users[jc % 2] = []
                        fsc_users[jc % 2].append(tev)
                        tsp = P.op("gpsimd", lambda e, jc=jc, tb=tb: e.dma_start(
                            out=ksp[(jc - 2) * 128:(jc - 1) * 128, tb * TB:(tb + 1) * TB],
                            in_=stgb[:, jc % 2, 0, :]), waits=[tev], incs=[("dSP", 16)])
                        stgb_users[jc % 2].append(tsp)
                        t_k_spill.append(tsp)

        if "v" in phases:
            t_wv = P.op("gpsimd", lambda e: e.dma_start(
                out=wgt[:].rearrange("p c j -> p (c j)"),
                in_=wvT.rearrange("(c p) j -> p c j", p=128)),
                waits=wgt_users, incs=[("dWG", 16)])
            t_wvsum = wsum_mms(t_wv)
            t_sp_all = t_sp_all_q
            t_mrl[0], t_rrl[0] = plan_mr_load(0)
            t_xload[0] = P.op("sync", lambda e: e.dma_start(
                out=xbv[:, 0, :, :].rearrange("p c t -> p (c t)"),
                in_=xbf_ap[0:128, :]),
                waits=[t_sp_all] + xb_users[0], incs=[("dXA", 16)])
            xb_users[0] = []
            for tb in range(NTB):
                sl = tb % 2
                if tb + 1 < NTB:
                    t_xload[tb + 1] = P.op("sync", lambda e, tb=tb: e.dma_start(
                        out=xbv[:, (tb + 1) % 2, :, :].rearrange("p c t -> p (c t)"),
                        in_=xbf_ap[(tb + 1) * 128:(tb + 2) * 128, :]),
                        waits=xb_users[(tb + 1) % 2],
                        incs=[("dXA" if (tb + 1) % 2 == 0 else "dXB", 16)])
                    xb_users[(tb + 1) % 2] = []
                    t_mrl[tb + 1], t_rrl[tb + 1] = plan_mr_load(tb + 1)
                trk4 = [None] * 4
                for tc in range(4):
                    for c in range(NDC):
                        P.op("tensor", lambda e, c=c, tc=tc, sl=sl: e.matmul(
                            qbank(tc), xbv[:, sl, c, tc * 128:(tc + 1) * 128], wgt[:, c, :],
                            start=(c == 0), stop=False),
                            waits=(psq4[tc] + [t_xload[tb], t_wv]) if c == 0 else [])
                    psq4[tc] = []
                ttr = None
                for tc in range(4):
                    ttr = P.op("tensor", lambda e, tc=tc, tb=tb: e.transpose(
                        ps6[:, tc:tc + 1], rrl[0:1, tb % 2, tc * 128:(tc + 1) * 128],
                        identf[0:1, 0:1]),
                        waits=([t_rrl[tb], t_identf] + ps6_users) if tc == 0 else [],
                        incs=[("sPE", 1)] if tc == 3 else [])
                ps6_users = []
                rrl_users[tb % 2].append(ttr)
                trc = P.op("vector", lambda e: e.tensor_copy(out=rcols[:], in_=ps6[:, 0:4]),
                           waits=[ttr] + rcol_users, incs=[("sV", 1)])
                rcol_users = []
                ps6_users.append(trc)
                for tc in range(4):
                    trk4[tc] = P.op("tensor", lambda e, tc=tc, tb=tb: e.matmul(
                        qbank(tc), mrl[0:1, tb % 2, tc * 128:(tc + 1) * 128], wsum_neg[:],
                        start=False, stop=True),
                        waits=[t_mrl[tb], t_wvsum], incs=[("sPE", 1)])
                    mrl_users[tb % 2].append(trk4[tc])
                xb_users[sl].append(trk4[3])
                wgt_users = [trk4[3]]
                for tc in range(4):
                    ti = tb * 4 + tc
                    tv = P.op("vector", lambda e, tc=tc: e.tensor_scalar(
                        out=stgb[:, tc % 2, :, :].rearrange("p a t -> p (a t)")[:, 0:512],
                        in0=qbank(tc), scalar1=rcols[:, tc:tc + 1], scalar2=None,
                        op0=ALU.mult),
                        waits=[trk4[tc], trc] + stgb_users[tc % 2], incs=[("sV", 1)])
                    stgb_users[tc % 2] = []
                    rcol_users.append(tv)
                    psq4[tc].append(tv)
                    for hh in range(2):
                        tsp = P.op("gpsimd", lambda e, ti=ti, tc=tc, hh=hh: e.dma_start(
                            out=vsp[hh * 128:hh * 128 + 128, ti * 256:(ti + 1) * 256],
                            in_=stgb[:, tc % 2, 0, hh * 256:(hh + 1) * 256]),
                            waits=[tv], incs=[("dSP", 16)])
                        stgb_users[tc % 2].append(tsp)
                        t_v_spill.append(tsp)
            psq_users = [psq4[0] + psq4[2], psq4[1] + psq4[3]]
            ps4_users = [psq4[2], psq4[3]]

        t_a2a_st = []
        exp_tok = {}
        prev_evac = list(ps4_users[0]) + list(ps4_users[1])
        ps4_users = [[], []]
        stg2_users = [[], []]
        last_exp = None
        last_avpe = None
        t_cc1 = None
        t_cc2 = None
        st_half = [[], []]
        n_half0 = HPC * 2 * (T // HTC) // 2 if T % (2 * HTC) == 0 else None
        if "p2" in phases:
            t_sp_all2 = ("dSP", P.counts.get("dSP", 0), len(P.items) - 1)
            srow_users += mrl_users[1] + rrl_users[1]
            mrow_users += mrl_users[0]
            rrow_users += rrl_users[0]
            mrl_users = [[], []]
            rrl_users = [[], []]
            t_wbbf = []
            for i4 in range(4):
                t_wbbf.append(P.op("gpsimd", lambda e, i4=i4: e.dma_start(
                    out=wbbf[:, i4 * 1024:(i4 + 1) * 1024],
                    in_=wbT[:, i4 * 1024:(i4 + 1) * 1024]), incs=[("dC3", 16)]))
            ar1_free = list(xb_users[0]) + list(xb_users[1])
            xb_users = [[], []]
            seedS = [pstat_users + [tv for tv in []], psq_users[0] + psq_users[1]]
            pstat_users = []
            psq_users = [[], []]
            for h in range(HPC):
                if h == 1:
                    tlq = P.op("sync", lambda e: e.dma_start(
                        out=qeh[:, :, :],
                        in_=qeffsp.rearrange("(a p) t -> p a t", p=128)),
                        waits=[t_sp_all2, last_avpe], incs=[("dP2", 16)])
                    tlk = P.op("sync", lambda e: e.dma_start(
                        out=kh[:, :, :],
                        in_=ksp.rearrange("(a p) t -> p a t", p=128)),
                        waits=[t_sp_all2, last_avpe], incs=[("dP2", 16)])
                tlv = P.op("sync", lambda e, h=h: e.dma_start(
                    out=vh[:, :, :].rearrange("p m j -> p (m j)"),
                    in_=vsp[h * 128:h * 128 + 128, :]),
                    waits=[t_sp_all2] + (wgt_users if h == 0 else [last_avpe]),
                    incs=[("dP2", 16)])
                res = ([t_qh0, t_kh0, tlv] if h == 0 else [tlq, tlk, tlv])
                qb_order = list(range(0, NQB, 2)) + list(range(1, NQB, 2))
                for qi, qb in enumerate(qb_order):
                    q0 = qb * TB
                    tAV_last = None
                    first_av = True
                    for g in range(NG + 1):
                        if g < NG:
                            w0 = list(res)
                            if g >= 2:
                                w0.append(exp_tok[(h, qb, g - 2)])
                            elif qi > 0:
                                w0.append(exp_tok[(h, qb_order[qi - 1], NG - 2 + g)])
                            elif h > 0:
                                w0.append(exp_tok[(h - 1, qb_order[-1], NG - 2 + g)])
                            else:
                                w0 += seedS[g % 2]
                            tSl = None
                            for mi in range(2):
                                mc = 2 * g + mi
                                for jj in range(2):
                                    tSl = P.op("tensor",
                                               lambda e, mc=mc, jj=jj, g=g, q0=q0: e.matmul(
                                                   ps0[:, (g % 2) * 1024 + (mc % 2) * 512:
                                                       (g % 2) * 1024 + (mc % 2) * 512 + 512],
                                                   kh[:, jj, mc * 128:(mc + 1) * 128],
                                                   qeh[:, jj, q0:q0 + TB],
                                                   start=(jj == 0), stop=(jj == 1)),
                                               waits=w0 if (mi == 0 and jj == 0) else [],
                                               incs=[("sPE", 1)] if (mi == 1 and jj == 1) else [])
                            wexp = [tSl]
                            if h == 0 and qb == 0 and g < 2:
                                wexp += ar1_free
                            texp = P.op("scalar", lambda e, g=g: e.activation(
                                out=pbv[:, 2 * g:2 * g + 2, :].rearrange("p a t -> p (a t)"),
                                in_=ps0[:, (g % 2) * 1024:(g % 2) * 1024 + 1024],
                                func=AF.Exp, scale=1.0),
                                waits=wexp, incs=[("sSC", 1)])
                            exp_tok[(h, qb, g)] = texp
                            last_exp = texp
                        ga = g - 1
                        if ga >= 0:
                            w1 = [exp_tok[(h, qb, ga)]]
                            if first_av:
                                w1 += prev_evac + ps6_users
                                prev_evac = []
                                ps6_users = []
                            for mi in range(2):
                                mc = 2 * ga + mi
                                st_ = bool(first_av and mi == 0)
                                sp_ = bool(ga == NG - 1 and mi == 1)
                                for ic in range(2):
                                    P.op("tensor",
                                         lambda e, mc=mc, ic=ic, st_=st_, sp_=sp_: e.matmul(
                                             ps4[:, ic * 512:(ic + 1) * 512],
                                             vh[:, mc, ic * 128:(ic + 1) * 128],
                                             pbv[:, mc, :],
                                             start=st_, stop=sp_),
                                         waits=w1 if (mi == 0 and ic == 0) else [])
                                tAV_last = P.op("tensor",
                                                lambda e, mc=mc, st_=st_, sp_=sp_: e.matmul(
                                                    ps6[0:1, 0:512], onesb[:], pbv[:, mc, :],
                                                    start=st_, stop=sp_),
                                                incs=[("sPE", 1)] if sp_ else [])
                            first_av = False
                    last_avpe = tAV_last
                    trs = P.op("vector", lambda e: e.tensor_copy(
                        out=srow0[0:1, 0:512], in_=ps6[0:1, 0:512]),
                        waits=[tAV_last] + srow_users, incs=[("sV", 1)])
                    srow_users = []
                    ps6_users.append(trs)
                    tln = P.op("scalar", lambda e: e.activation(
                        out=srow1[0:1, 0:512], in_=srow0[0:1, 0:512], func=AF.Ln),
                        waits=[trs], incs=[("sSC", 1)])
                    tinv = P.op("scalar", lambda e: e.activation(
                        out=srow0[0:1, 0:512], in_=srow1[0:1, 0:512], func=AF.Exp,
                        scale=-1.0), waits=[tln], incs=[("sSC", 1)])
                    srow_users = [tinv]
                    trb2, tibc = bcast(tinv, srow0[0:1, 0:512], rbc2[:], 512)
                    srow_users.append(trb2)
                    for ic in range(2):
                        tmul = P.op("vector", lambda e, ic=ic: e.tensor_tensor(
                            out=fsc[:, ic, :], in0=ps4[:, ic * 512:(ic + 1) * 512],
                            in1=rbc2[:], op=ALU.mult),
                            waits=[tibc, tAV_last] + fsc_users[ic], incs=[("sV", 1)])
                        fsc_users[ic] = []
                        prev_evac.append(tmul)
                        tbias = P.op("scalar", lambda e, ic=ic, h=h: e.activation(
                            out=stgb[:, ic, 1, :], in_=fsc[:, ic, :],
                            func=AF.Identity, bias=bvs[:, h * 2 + ic:h * 2 + ic + 1],
                            scale=1.0),
                            waits=[tmul, t_bv] + stg2_users[ic], incs=[("sSC", 1)])
                        stg2_users[ic] = []
                        fsc_users[ic].append(tbias)
                        for seg in range(q0 // HTC, (q0 + TB) // HTC):
                            tlo = seg * HTC
                            j = tlo // TC
                            hf = (tlo - j * TC) // HTC
                            tst = P.op("sync",
                                       lambda e, ic=ic, h=h, j=j, hf=hf, tlo=tlo, q0=q0:
                                       e.dma_start(
                                           out=a2a_in_aps[hf][
                                               j * JCORE + h * 256 + ic * 128:
                                               j * JCORE + h * 256 + (ic + 1) * 128, :],
                                           in_=stgb[:, ic, 1, tlo - q0:tlo - q0 + HTC]),
                                       waits=[tbias], incs=[("dSP", 16)])
                            t_a2a_st.append(tst)
                            st_half[hf].append(tst)
                            stg2_users[ic].append(tst)
                        if t_cc1 is None and len(st_half[0]) == n_half0:
                            def emit_cc1(e):
                                with allow_ici_all_to_all():
                                    return e.collective_compute(
                                        "AllToAll", ALU.bypass,
                                        replica_groups=[list(range(NCORE))],
                                        ins=[a2a_in1.ap().opt()],
                                        outs=[a2a_out1.ap().opt()])
                            t_cc1 = P.op("gpsimd", emit_cc1, waits=st_half[0],
                                         incs=[("sCC", 1)])

        if "cc" in phases:
            if t_cc1 is None:
                def emit_cc1b(e):
                    with allow_ici_all_to_all():
                        return e.collective_compute(
                            "AllToAll", ALU.bypass,
                            replica_groups=[list(range(NCORE))],
                            ins=[a2a_in1.ap().opt()],
                            outs=[a2a_out1.ap().opt()])
                t_cc1 = P.op("gpsimd", emit_cc1b, waits=st_half[0], incs=[("sCC", 1)])

            def emit_cc2(e):
                with allow_ici_all_to_all():
                    return e.collective_compute(
                        "AllToAll", ALU.bypass,
                        replica_groups=[list(range(NCORE))],
                        ins=[a2a_in2.ap().opt()],
                        outs=[a2a_out2.ap().opt()])
            t_cc2 = P.op("gpsimd", emit_cc2, waits=st_half[1], incs=[("sCC", 1)])

        if "p3" in phases:
            w_ar3 = [last_avpe] if last_avpe else []
            t_aT = P.op("gpsimd", lambda e: e.dma_start(
                out=aTb[:].rearrange("p c j -> p (c j)"),
                in_=aT.rearrange("(c p) j -> p c j", p=128)),
                waits=w_ar3, incs=[("dC3", 16)])
            t_b2d = P.op("gpsimd", lambda e: e.dma_start(out=b2dTb[:], in_=b2dT),
                         incs=[("dC3", 16)])
            t_wr = P.op("gpsimd", lambda e: e.dma_start(
                out=wrTb[:].rearrange("p c j -> p (c j)"),
                in_=wrT.rearrange("(c p) j -> p c j", p=128)), incs=[("dC3", 16)])
            t_c3all = ("dC3", P.counts.get("dC3", 0), len(P.items) - 1)
            rbcB = stgb[:, 0, 0, :]
            rbc2B = stgb[:, 0, 1, :]
            SGR = min(128, HTC)
            NSUB = TB3 // SGR
            xs_users = []
            stgb3_users = stg2_users[0] + stg2_users[1]
            x2t_users = [[], []]
            rl_users = []
            wem_users = []
            hs_users = []
            hwb_users = [[], []]
            hw_tok = [None] * NT3
            x2ln_tok = [None] * NT3
            stat_u = [last_exp] if last_exp else []
            pe_base_users = [[last_exp] if last_exp else [],
                             [last_exp] if last_exp else []]
            ar1p3 = [last_avpe] if last_avpe else []
            wb_slab_users = [[], []]
            t_wb = [None, None]
            ev1_users = []
            ost_users = [[], []]
            NSLAB = D // 256
            passA_tok = [None] * NT3

            x2spv = x2sp.rearrange("p (c t) -> p c t", t=TC)
            NCG = NDC // 8
            x_slv = x_sl.rearrange("p (c t) -> p c t", t=TC)

            def plan_passA(tb3):
                nonlocal ar1p3, sq_users, xs_users
                tk0 = tb3 * TB3
                tx = P.op("gpsimd", lambda e, tk0=tk0: e.dma_start(
                    out=x2v[:, :, tk0:tk0 + TB3],
                    in_=x_slv[:, :, tk0:tk0 + TB3]),
                    waits=(ar1p3 if tb3 == 0 else []) + xs_users, incs=[("dXIA", 16)])
                if tb3 == 0:
                    ar1p3 = []
                xs_users = []
                hf0 = tk0 // HTC
                hf1 = (tk0 + TB3 - 1) // HTC
                tlast = tx
                for hf in range(hf0, hf1 + 1):
                    c0 = max(tk0, hf * HTC)
                    c1_ = min(tk0 + TB3, (hf + 1) * HTC)
                    tlast = P.op("gpsimd", lambda e, hf=hf, c0=c0, c1_=c1_: e.dma_start(
                        out=x2v[:, :, c0:c1_],
                        in_=a2a_out_aps[hf][:, c0 - hf * HTC:c1_ - hf * HTC]
                        .rearrange("(c p) t -> p c t", p=128),
                        accum_op=ALU.add),
                        waits=[tx, t_cc1 if hf == 0 else t_cc2],
                        incs=[("dAIA", 16)])
                xs_users.append(tlast)
                tsv = P.op("sync", lambda e, tk0=tk0: e.dma_start(
                    out=x2spv[:, :, tk0:tk0 + TB3],
                    in_=x2v[:, :, tk0:tk0 + TB3]),
                    waits=[tlast], incs=[("dSP", 16)])
                tsaves = [tsv]
                tsq = None
                for c in range(NDC):
                    tsq = P.op("vector", lambda e, c=c, tk0=tk0: e.tensor_tensor(
                        out=sq3[:, c, :], in0=x2v[:, c, tk0:tk0 + TB3],
                        in1=x2v[:, c, tk0:tk0 + TB3], op=ALU.mult),
                        waits=([tlast] + sq_users +
                               ([last_avpe] if (tb3 == 0 and last_avpe) else []))
                        if c == 0 else [],
                        incs=[("sV", 1)] if c == NDC - 1 else [])
                sq_users = []
                passA_tok[tb3] = (tlast, tsaves, tsq)

            plan_passA(0)
            for tb3 in range(NT3):
                tk0 = tb3 * TB3
                tlastadd, tsaves, tsq = passA_tok[tb3]
                w0 = stat_u + pstat_users + [tlastadd, t_ones]
                pstat_users = []
                stat_u = []
                for c in range(NDC):
                    P.op("tensor", lambda e, c=c, tk0=tk0: e.matmul(
                        ps0[0:1, 0:TB3], onesb[:], x2v[:, c, tk0:tk0 + TB3],
                        start=(c == 0), stop=(c == NDC - 1)),
                        waits=w0 if c == 0 else [])
                tstat = None
                for c in range(NDC):
                    tstat = P.op("tensor", lambda e, c=c: e.matmul(
                        ps0[0:1, 512:512 + TB3], onesb[:], sq3[:, c, :],
                        start=(c == 0), stop=(c == NDC - 1)),
                        waits=[tsq] if c == 0 else [],
                        incs=[("sPE", 1)] if c == NDC - 1 else [])
                mrow_users.append(tstat)
                tm, tr2 = row_chain(TB3)
                tmr = P.op("vector", lambda e: e.tensor_tensor(
                    out=srow1[0:1, 0:TB3], in0=mrow[0:1, 0:TB3], in1=rrow[0:1, 0:TB3],
                    op=ALU.mult), waits=[tr2, tm], incs=[("sV", 1)])
                mrow_users.append(tmr)
                rrow_users.append(tmr)
                srow_users.append(tmr)
                trb3, trbc3 = bcast(tr2, rrow[0:1, 0:TB3], rbcB[:, 0:TB3], TB3,
                                    extra=stgb3_users)
                rrow_users.append(trb3)
                trb4, tmbc3 = bcast(tmr, srow1[0:1, 0:TB3], rbc2B[:, 0:TB3], TB3,
                                    extra=stgb3_users)
                stgb3_users = []
                srow_users.append(trb4)
                tlast = None
                for c in range(NDC):
                    t1_ = P.op("vector", lambda e, c=c, tk0=tk0: e.tensor_tensor(
                        out=qsc[:, 0, 0:TB3], in0=x2v[:, c, tk0:tk0 + TB3],
                        in1=rbcB[:, 0:TB3], op=ALU.mult),
                        waits=([trbc3, tsq] + tsaves + qsc_users[0]) if c == 0 else [],
                        incs=[("sV", 1)])
                    qsc_users[0] = []
                    tlast = P.op("vector", lambda e, c=c, tk0=tk0: e.tensor_tensor(
                        out=x2v[:, c, tk0:tk0 + TB3], in0=qsc[:, 0, 0:TB3],
                        in1=rbc2B[:, 0:TB3], op=ALU.subtract),
                        waits=[t1_, tmbc3] if c == 0 else [t1_], incs=[("sV", 1)])
                    qsc_users[0] = [tlast]
                x2ln_tok[tb3] = tlast
                trt = None
                for c in range(NDC):
                    trt = P.op("tensor", lambda e, c=c, tk0=tk0: e.matmul(
                        ps6[0:8, 0:TB3], wrTb[:, c, :], x2v[:, c, tk0:tk0 + TB3],
                        start=(c == 0), stop=(c == NDC - 1)),
                        waits=([x2ln_tok[tb3], t_c3all] + ps6_users) if c == 0 else [],
                        incs=[("sPE", 1)] if c == NDC - 1 else [])
                ps6_users = []
                trl = P.op("vector", lambda e: e.tensor_copy(
                    out=rl[:, 0:TB3], in_=ps6[0:8, 0:TB3]),
                    waits=[trt] + rl_users, incs=[("sV", 1)])
                rl_users = []
                ps6_users.append(trl)
                th = None
                for c in range(NDC):
                    th = P.op("tensor", lambda e, c=c, tb3=tb3, tk0=tk0: e.matmul(
                        ps4[:, (tb3 % 2) * 512:(tb3 % 2) * 512 + TB3],
                        aTb[:, c, :], x2v[:, c, tk0:tk0 + TB3],
                        start=(c == 0), stop=(c == NDC - 1)),
                        waits=([x2ln_tok[tb3], t_c3all] + ps4_users[tb3 % 2] + prev_evac)
                        if c == 0 else [],
                        incs=[("sPE", 1)] if c == NDC - 1 else [])
                prev_evac = []
                ps4_users[tb3 % 2] = []
                ths = P.op("vector", lambda e, tb3=tb3: e.tensor_copy(
                    out=hs[:, 0:TB3], in_=ps4[:, (tb3 % 2) * 512:(tb3 % 2) * 512 + TB3]),
                    waits=[th] + hs_users, incs=[("sV", 1)])
                hs_users = []
                ps4_users[tb3 % 2].append(ths)
                wem_tok = None
                for tc in range(NTC3):
                    tt1 = P.op("tensor", lambda e, tc=tc: e.transpose(
                        ps7[:, tc * 8:(tc + 1) * 8], rl[:, tc * 128:(tc + 1) * 128],
                        identf[0:8, 0:8]),
                        waits=[trl, t_identf] + ps7_users,
                        incs=[("sPE", 1)])
                    ps7_users = []
                    rl_users.append(tt1)
                    tlt = P.op("vector", lambda e, tc=tc: e.tensor_copy(
                        out=lt8[:], in_=ps7[:, tc * 8:(tc + 1) * 8]),
                        waits=[tt1], incs=[("sV", 1)])
                    ps7_users.append(tlt)
                    tm1 = P.op("vector", lambda e: e.tensor_reduce(
                        out=c1[:], in_=lt8[:], axis=AX.X, op=ALU.max),
                        waits=[tlt], incs=[("sV", 1)])
                    tg1 = P.op("vector", lambda e: e.tensor_scalar(
                        out=ge1[:], in0=lt8[:], scalar1=c1[:, 0:1], scalar2=None,
                        op0=ALU.is_ge), waits=[tm1], incs=[("sV", 1)])
                    tl2 = P.op("vector", lambda e: e.scalar_tensor_tensor(
                        out=l28[:], in0=ge1[:], scalar=-1e9, in1=lt8[:],
                        op0=ALU.mult, op1=ALU.add), waits=[tg1], incs=[("sV", 1)])
                    tm2 = P.op("vector", lambda e: e.tensor_reduce(
                        out=c2[:], in_=l28[:], axis=AX.X, op=ALU.max),
                        waits=[tl2], incs=[("sV", 1)])
                    tg2 = P.op("vector", lambda e: e.tensor_scalar(
                        out=ge2[:], in0=l28[:], scalar1=c2[:, 0:1], scalar2=None,
                        op0=ALU.is_ge), waits=[tm2], incs=[("sV", 1)])
                    td = P.op("vector", lambda e: e.tensor_scalar(
                        out=c3[:], in0=c2[:], scalar1=c1[:, 0:1], scalar2=None,
                        op0=ALU.subtract), waits=[tm2], incs=[("sV", 1)])
                    tex = P.op("scalar", lambda e: e.activation(
                        out=c3[:], in_=c3[:], func=AF.Exp), waits=[td], incs=[("sSC", 1)])
                    tsum = P.op("vector", lambda e: e.tensor_scalar(
                        out=c4[:], in0=c3[:], scalar1=1.0, scalar2=None, op0=ALU.add),
                        waits=[tex], incs=[("sV", 1)])
                    tinv = P.op("vector", lambda e: e.reciprocal(out=c4[:], in_=c4[:]),
                                waits=[tsum], incs=[("sV", 1)])
                    tbw = P.op("vector", lambda e: e.tensor_scalar(
                        out=c3[:], in0=c3[:], scalar1=c4[:, 0:1], scalar2=None,
                        op0=ALU.mult), waits=[tinv], incs=[("sV", 1)])
                    tw1 = P.op("vector", lambda e: e.tensor_scalar(
                        out=wtm[:], in0=ge1[:], scalar1=c4[:, 0:1], scalar2=None,
                        op0=ALU.mult), waits=[tinv, tg1], incs=[("sV", 1)])
                    tw2 = P.op("vector", lambda e: e.scalar_tensor_tensor(
                        out=wtm[:], in0=ge2[:], scalar=c3[:, 0:1], in1=wtm[:],
                        op0=ALU.mult, op1=ALU.add),
                        waits=[tw1, tbw, tg2], incs=[("sV", 1)])
                    tt2 = P.op("tensor", lambda e, tc=tc: e.transpose(
                        ps6[0:8, tc * 128:(tc + 1) * 128], wtm[:], identf[:]),
                        waits=[tw2] + ps6_users, incs=[("sPE", 1)])
                    ps6_users = []
                    wem_tok = P.op("vector", lambda e, tc=tc: e.tensor_copy(
                        out=wem[:, tc * 128:(tc + 1) * 128],
                        in_=ps6[0:8, tc * 128:(tc + 1) * 128]),
                        waits=[tt2] + (wem_users if tc == 0 else []), incs=[("sV", 1)])
                    if tc == 0:
                        wem_users = []
                    ps6_users.append(wem_tok)
                tE = P.op("tensor", lambda e: e.matmul(
                    ps7[:, 0:TB3], e2b[:], wem[:, 0:TB3], start=True, stop=True),
                    waits=[wem_tok, t_e2] + ps7_users, incs=[("sPE", 1)])
                ps7_users = []
                wem_users.append(tE)
                hw_tok[tb3] = P.op("vector", lambda e, tb3=tb3: e.tensor_tensor(
                    out=hwb[:, tb3, :], in0=hs[:, 0:TB3], in1=ps7[:, 0:TB3], op=ALU.mult),
                    waits=[tE, ths] + hwb_users[tb3], incs=[("sV", 1)])
                hwb_users[tb3] = []
                hs_users.append(hw_tok[tb3])
                ps7_users.append(hw_tok[tb3])
                if tb3 + 1 < NT3:
                    plan_passA(tb3 + 1)
                t_wb[0] = P.op("scalar", lambda e, tb3=tb3: e.dma_start(
                    out=wbs[:, 0, :, :],
                    in_=wbbf[:, 0:256].rearrange("(c p) j -> p c j", p=128)),
                    waits=[t_c3all] + wb_slab_users[0] +
                          (xb_users[0] + xb_users[1] if tb3 == 0 else []),
                    incs=[("dWBA", 16)])
                if tb3 == 0:
                    xb_users = [[], []]
                wb_slab_users[0] = []
                for osl in range(NSLAB):
                    slp = osl % 2
                    if osl + 1 < NSLAB:
                        t_wb[(osl + 1) % 2] = P.op("scalar", lambda e, osl=osl: e.dma_start(
                            out=wbs[:, (osl + 1) % 2, :, :],
                            in_=wbbf[:, (osl + 1) * 256:(osl + 2) * 256].rearrange(
                                "(c p) j -> p c j", p=128)),
                            waits=wb_slab_users[(osl + 1) % 2],
                            incs=[("dWBA" if (osl + 1) % 2 == 0 else "dWBB", 16)])
                        wb_slab_users[(osl + 1) % 2] = []
                    for osub in range(2):
                        oc = osl * 2 + osub
                        par = oc % 2
                        bank = 2 + par
                        po = ps0[:, bank * 512:bank * 512 + TB3]
                        tx2 = P.op("scalar", lambda e, oc=oc, tk0=tk0, par=par: e.dma_start(
                            out=qsc[:, 1, 0:TB3] if par else stgb[:, 1, 0, 0:TB3],
                            in_=x2spv[:, oc, tk0:tk0 + TB3]),
                            waits=tsaves + x2t_users[par] +
                                  (stgb3_users if (tb3 == 0 and oc < 2) else []),
                            incs=[("dMR" if par == 0 else "dRR", 16)])
                        x2t_users[par] = []
                        tmain = None
                        for c in range(NDC):
                            tmain = P.op("tensor",
                                         lambda e, c=c, osub=osub, tk0=tk0, po=po, slp=slp:
                                         e.matmul(
                                             po, wbs[:, slp, c, osub * 128:(osub + 1) * 128],
                                             x2v[:, c, tk0:tk0 + TB3],
                                             start=(c == 0), stop=False),
                                         waits=(pe_base_users[par] +
                                                [t_wb[slp], x2ln_tok[tb3]]) if c == 0 else [])
                        pe_base_users[par] = []
                        tlora = P.op("tensor", lambda e, oc=oc, tb3=tb3, po=po: e.matmul(
                            po, b2dTb[:, oc * 128:(oc + 1) * 128], hwb[:, tb3, :],
                            start=False, stop=True),
                            waits=[hw_tok[tb3], t_c3all], incs=[("sPE", 1)])
                        hwb_users[tb3].append(tlora)
                        if osub == 1:
                            wb_slab_users[slp].append(tlora)
                        tb1 = P.op("scalar", lambda e, oc=oc, po=po: e.activation(
                            out=fsc[:, 1, 0:TB3], in_=po, func=AF.Identity,
                            bias=bbs[:, oc:oc + 1], scale=1.0),
                            waits=[tlora, t_bb] + fsc_users[1] + ev1_users,
                            incs=[("sSC", 1)])
                        fsc_users[1] = []
                        ev1_users = []
                        pe_base_users[par].append(tb1)
                        tadd2 = P.op("vector", lambda e, par=par: e.tensor_tensor(
                            out=ost[:, par, 0:TB3], in0=fsc[:, 1, 0:TB3],
                            in1=qsc[:, 1, 0:TB3] if par else stgb[:, 1, 0, 0:TB3],
                            op=ALU.add),
                            waits=[tb1, tx2] + ost_users[par],
                            incs=[("sV", 1)])
                        ost_users[par] = []
                        ev1_users.append(tadd2)
                        x2t_users[par].append(tadd2)
                        tst = P.op("sync", lambda e, oc=oc, tk0=tk0, par=par: e.dma_start(
                            out=out[oc * 128:(oc + 1) * 128, tk0:tk0 + TB3],
                            in_=ost[:, par, 0:TB3]), waits=[tadd2], incs=[("dOUT", 16)])
                        ost_users[par].append(tst)

        def _dma_tap(name, src_ap):
            if name in taps:
                P.op("sync", lambda e, s=src_ap, d=tap_aps[name]: e.dma_start(out=d, in_=s),
                     waits=[(s_, v_, len(P.items) - 1) for s_, v_ in P.counts.items()],
                     incs=[("dSP", 16)])
        _dma_tap("qeffsp", qeffsp)
        _dma_tap("ksp", ksp)
        _dma_tap("vsp", vsp)
        _dma_tap("mrows", mr_rows)
        _dma_tap("xbf", xbf_ap)
        _dma_tap("x2sp", x2sp)

        fin = [(s, v, len(P.items) - 1) for s, v in P.counts.items()]
        P.op("sync", None, waits=fin)

        with nc.Block() as block:
            @block.sync
            def _(e):
                P.emit("sync", e, sems)

            @block.tensor
            def _(e):
                P.emit("tensor", e, sems)

            @block.vector
            def _(e):
                P.emit("vector", e, sems)

            @block.scalar
            def _(e):
                P.emit("scalar", e, sems)

            @block.gpsimd
            def _(e):
                P.emit("gpsimd", e, sems)

    return nc


def _host_stage(inputs, T=8192):
    d0, kcx = _ga_consts()
    x = np.ascontiguousarray(inputs["L0_attn_x"][0].T)
    wq, wk, wv = inputs["L0_attn_Wq"], inputs["L0_attn_Wk"], inputs["L0_attn_Wv"]
    bq, bk, bv = inputs["L0_attn_bq"], inputs["L0_attn_bk"], inputs["L0_attn_bv"]
    bw = inputs["L0_attn_bias_weight"]
    wb, bb = inputs["L0_ffn_W_base"], inputs["L0_ffn_b_base"]
    wr, A, B = inputs["L0_ffn_W_router"], inputs["L0_ffn_A"], inputs["L0_ffn_B"]
    TC = T // NCORE
    d0kc = np.concatenate([d0, kcx[0], kcx[1], kcx[2]], axis=1)
    identF = np.eye(128, dtype=np.float32)
    e2 = np.zeros((8, 128), np.float32)
    for er in range(128):
        e2[er // 16, er] = 2.0
    wbT = np.ascontiguousarray(wb.T)
    aT = np.ascontiguousarray(np.asarray(A).reshape(128, D).T)
    b2dT = np.ascontiguousarray(np.asarray(B).transpose(1, 0, 2).reshape(D, 128).T)
    wrT = np.ascontiguousarray(np.asarray(wr).T)
    bbc = np.ascontiguousarray(np.asarray(bb).reshape(NDC, 128).T)
    NTB = T // TB
    x_pm = np.ascontiguousarray(
        x.reshape(NDC, 128, NTB, TB).transpose(2, 1, 0, 3).reshape(NTB * 128, NDC * TB))
    in_maps = []
    for c in range(NCORE):
        js = slice(c * JCORE, (c + 1) * JCORE)
        xs_ = x[:, c * TC:(c + 1) * TC]
        im = {
            "x_fm": x_pm,
            "x_sl": np.ascontiguousarray(
                xs_.reshape(NDC, 128, TC).transpose(1, 0, 2).reshape(128, NDC * TC)),
            "wqT": np.ascontiguousarray(np.asarray(wq).T[:, js]),
            "wkT": np.ascontiguousarray(np.asarray(wk).T[:, js]),
            "wvT": np.ascontiguousarray(np.asarray(wv).T[:, js]),
            "bqc": np.ascontiguousarray(np.asarray(bq)[js].reshape(4, 128).T),
            "bkc": np.ascontiguousarray(np.asarray(bk)[js].reshape(4, 128).T),
            "bvc": np.ascontiguousarray(np.asarray(bv)[js].reshape(4, 128).T),
            "bwt": np.ascontiguousarray(np.asarray(bw)[c * HPC:(c + 1) * HPC].reshape(1, HPC * 3)),
            "wbT": wbT,
            "bbc": bbc,
            "wrT": wrT,
            "aT": aT,
            "b2dT": b2dT,
            "d0kc": d0kc,
            "identF": identF,
            "e2c": e2,
        }
        in_maps.append({k: np.asarray(v, dtype=np.float32) for k, v in im.items()})
    return in_maps


def kernel(**inputs):
    T = 8192
    in_maps = _host_stage(inputs, T)
    nc = build(T)
    res = run_bass_kernel_spmd(nc, in_maps, core_ids=list(range(NCORE)))
    full = np.concatenate([res.results[c]["out"] for c in range(NCORE)], axis=1)
    return np.ascontiguousarray(full.T).reshape(1, T, D).astype(np.float32)
